# version 10
# baseline (speedup 1.0000x reference)
import numpy as np
import ml_dtypes
from contextlib import ExitStack

import concourse.bass as bass
import concourse.mybir as mybir
from concourse.bass_utils import run_bass_kernel_spmd

F32 = mybir.dt.float32
BF16 = mybir.dt.bfloat16
F16 = mybir.dt.float16
I32 = mybir.dt.int32
U16 = mybir.dt.uint16
AF = mybir.ActivationFunctionType
ALU = mybir.AluOpType
AX = mybir.AxisListType
BF = ml_dtypes.bfloat16

B, S, D = 4, 2048, 2048
NH, DKH, DHR, LAT = 16, 128, 64, 512
IFF = 5632
NT = 8
TPC = NT * 128
NCH = D // 128
GU_SH = 1408
DN_SH = 256
NEG = -30000.0
RSQ_DK = float(1.0 / np.sqrt(128.0))

UC = [max(len([t for t in range(p, 16, 2) if t >= k]) for p in (0, 1)) for k in range(16)]
PAIRS = [[0, 1], [2, 3], [4, 5], [6, 7]]
ALL8 = [list(range(8))]

_UPTO = "all"
_TAP = None
_ORD = ["x", "l0attn", "l0ffn", "l1attn", "all"]



def _rope_l0_tables(pos):
    rot = 1024
    inv = 1.0 / (10000.0 ** (np.arange(0, rot, 2, dtype=np.float64) / rot))
    fr = np.repeat(pos[:, None].astype(np.float64) * inv[None, :], 2, axis=-1)
    return (np.ascontiguousarray(np.cos(fr).T.astype(BF)),
            np.ascontiguousarray(np.sin(fr).T.astype(BF)))


def _rope_l1_tables(pos, d):
    pe = np.repeat(np.arange(d // 2, dtype=np.float64), 2)
    th = np.exp(-2.0 * pe / d * np.log(10000.0))
    ang = (pos[:, None].astype(np.float64) + 1.0) * th[None, :]
    return (np.ascontiguousarray(np.cos(ang).T.astype(BF)),
            np.ascontiguousarray(np.sin(ang).T.astype(BF)))


def _rot_sign_mat():
    R = np.zeros((128, 128), dtype=np.float32)
    for i in range(64):
        R[2 * i + 1, 2 * i] = -1.0
        R[2 * i, 2 * i + 1] = 1.0
    return np.ascontiguousarray(R.astype(BF))


def _bias_cols(vec, n128):
    return np.ascontiguousarray(np.asarray(vec, dtype=np.float32).reshape(n128, 128).T)


def _mk_masks(p):
    m = np.zeros((16, 128, 128), dtype=np.float32)
    tri = np.where(np.arange(128)[:, None] > np.arange(128)[None, :], NEG, 0.0)
    for k in range(16):
        own = len([t for t in range(p, 16, 2) if t >= k])
        if k % 2 == p:
            m[k] = tri
        elif UC[k] > own:
            m[k] = NEG
    return np.ascontiguousarray(m)


def _core_pos(c):
    p = c % 2
    return np.concatenate([np.arange(t * 128, t * 128 + 128) for t in range(p, 16, 2)])


def make_in_maps(inputs):
    X = np.asarray(inputs["L0_attn_X"], dtype=np.float32)
    W_inT = np.ascontiguousarray(np.asarray(inputs["L0_attn_W_in"], dtype=np.float32).T)
    W_outT = np.ascontiguousarray(np.asarray(inputs["L0_attn_W_out"], dtype=np.float32).T)
    gup = {0: np.asarray(inputs["L0_ffn_gate_up_packed"]), 1: np.asarray(inputs["L1_ffn_gate_up_packed"])}
    gus = {0: np.asarray(inputs["L0_ffn_gate_up_scales"]), 1: np.asarray(inputs["L1_ffn_gate_up_scales"])}
    dnp = {0: np.asarray(inputs["L0_ffn_down_packed"]), 1: np.asarray(inputs["L1_ffn_down_packed"])}
    dns = {0: np.asarray(inputs["L0_ffn_down_scales"]), 1: np.asarray(inputs["L1_ffn_down_scales"])}

    biases = np.concatenate([
        _bias_cols(inputs["L0_attn_b_in"], 48),
        _bias_cols(inputs["L0_attn_b_out"], 16),
        _bias_cols(inputs["L1_attn_W_down_b"], 8),
        _bias_cols(np.asarray(inputs["L1_attn_W_uKV_b"])[:2048], 16),
        _bias_cols(inputs["L1_attn_W_uQ_b"], 16),
        _bias_cols(inputs["L1_attn_Wqr_b"], 8),
        _bias_cols(inputs["L1_attn_Wout_b"], 16),
        np.pad(np.asarray(inputs["L1_attn_Wkr_b"], dtype=np.float32), (0, 64))[:, None],
        np.full((128, 1), 1e-5, dtype=np.float32),
        np.full((128, 1), 1e-6, dtype=np.float32),
    ], axis=1).astype(np.float32)

    vbias = np.ascontiguousarray(
        np.asarray(inputs["L1_attn_W_uKV_b"])[2048:].astype(BF)[None, :])
    ident = np.eye(128, dtype=np.float32)
    Rm = _rot_sign_mat()
    coskF, sinkF = _rope_l1_tables(np.arange(S), 64)

    in_maps = []
    for c in range(8):
        b, p = c // 2, c % 2
        pos = _core_pos(c)
        cos0, sin0 = _rope_l0_tables(pos)
        cosq, sinq = _rope_l1_tables(pos, 1024)
        im = {
            "x": np.ascontiguousarray(X[b, pos, :]),
            "w_inT": W_inT, "w_outT": W_outT,
            "biases": np.ascontiguousarray(biases),
            "vbias": vbias, "ident": ident, "identb": ident.astype(BF), "rmat": Rm,
            "cos0": cos0, "sin0": sin0, "cosq": cosq, "sinq": sinq,
            "cosk": coskF, "sink": sinkF,
            "masks": _mk_masks(p),
            "slot": np.array([[p]], dtype=np.int32),
            "wdn": np.ascontiguousarray(np.asarray(inputs["L1_attn_W_down_k"], dtype=np.float32)),
            "wukv": np.ascontiguousarray(np.asarray(inputs["L1_attn_W_uKV_k"], dtype=np.float32)),
            "wuq": np.ascontiguousarray(np.asarray(inputs["L1_attn_W_uQ_k"], dtype=np.float32)),
            "wkr": np.ascontiguousarray(np.asarray(inputs["L1_attn_Wkr_k"], dtype=np.float32)),
            "wqr": np.ascontiguousarray(np.asarray(inputs["L1_attn_Wqr_k"], dtype=np.float32)),
            "wo1": np.ascontiguousarray(np.asarray(inputs["L1_attn_Wout_k"], dtype=np.float32)),
        }
        for L in (0, 1):
            im[f"gup{L}"] = np.ascontiguousarray(gup[L][GU_SH * c:GU_SH * (c + 1)])
            im[f"gus{L}"] = np.ascontiguousarray(
                gus[L].T[GU_SH * c:GU_SH * (c + 1)].astype(np.float16))
            im[f"dnp{L}"] = np.ascontiguousarray(dnp[L][DN_SH * c:DN_SH * (c + 1)])
            im[f"dns{L}"] = np.ascontiguousarray(
                dns[L].T[DN_SH * c:DN_SH * (c + 1)].astype(np.float16))
        in_maps.append(im)
    return in_maps



class KB:

    DMA_POOL = {"SP": 10, "Pool": 4, "Activation": 2}

    def __init__(self, nc, es):
        self.nc = nc
        self.c = {}
        self.sems = {}
        for n in ("tS", "vS", "sS", "cc", "gS"):
            self.sems[n] = es.enter_context(nc.semaphore(n))
            self.c[n] = 0
        self.pool = {}
        self.rr = {}
        for eng, sz in self.DMA_POOL.items():
            self.pool[eng] = []
            self.rr[eng] = 0
            for i in range(sz):
                n = f"qd_{eng}_{i}"
                self.sems[n] = es.enter_context(nc.semaphore(n))
                self.c[n] = 0
                self.pool[eng].append(n)
        self.bank_free = [None] * 8

    def bump(self, name, inst, n=1):
        inst.then_inc(self.sems[name], n)
        self.c[name] += n
        return (name, self.c[name])

    def mark(self, name):
        return (name, self.c[name])

    def wait(self, eng, mk):
        if mk is None:
            return
        name, v = mk
        if v > 0:
            eng.wait_ge(self.sems[name], v)

    def dma(self, eng, qname, out, in_):
        ename = eng.engine.name if hasattr(eng, "engine") else "SP"
        pool = self.pool[ename]
        slot = pool[self.rr[ename] % len(pool)]
        self.rr[ename] += 1
        if self.c[slot] > 0:
            eng.wait_ge(self.sems[slot], self.c[slot])
        return self.bump(slot, eng.dma_start(out=out, in_=in_), 16)

    def use_bank(self, i):
        if self.bank_free[i] is not None:
            self.wait(self.nc.tensor, self.bank_free[i])
            self.bank_free[i] = None

    def evac_mark(self, i, mk):
        self.bank_free[i] = mk


def build(upto="all", tap=None):
    lvl = _ORD.index(upto)
    nc = bass.Bass("TRN2", target_bir_lowering=False)
    T, V, SC, SY, G = nc.tensor, nc.vector, nc.scalar, nc.sync, nc.gpsimd

    def din(name, shape, dt):
        return nc.dram_tensor(name, shape, dt, kind="ExternalInput").ap()

    x_d = din("x", [TPC, D], F32)
    w_inT_d = din("w_inT", [D, 3 * D], F32)
    w_outT_d = din("w_outT", [D, D], F32)
    biases_d = din("biases", [128, 131], F32)
    vbias_d = din("vbias", [1, D], BF16)
    ident_d = din("ident", [128, 128], F32)
    identb_d = din("identb", [128, 128], BF16)
    rmat_d = din("rmat", [128, 128], BF16)
    cos0_d, sin0_d = din("cos0", [1024, TPC], BF16), din("sin0", [1024, TPC], BF16)
    cosq_d, sinq_d = din("cosq", [1024, TPC], BF16), din("sinq", [1024, TPC], BF16)
    cosk_d, sink_d = din("cosk", [64, S], BF16), din("sink", [64, S], BF16)
    masks_d = din("masks", [16, 128, 128], F32)
    slot_d = din("slot", [1, 1], I32)
    wdn_d = din("wdn", [D, 1024], F32)
    wukv_d = din("wukv", [LAT, 4096], F32)
    wuq_d = din("wuq", [LAT, D], F32)
    wkr_d = din("wkr", [D, 64], F32)
    wqr_d = din("wqr", [D, 1024], F32)
    wo1_d = din("wo1", [D, D], F32)
    gup_d = {L: din(f"gup{L}", [GU_SH, 256], I32) for L in (0, 1)}
    gus_d = {L: din(f"gus{L}", [GU_SH, 16], F16) for L in (0, 1)}
    dnp_d = {L: din(f"dnp{L}", [DN_SH, 704], I32) for L in (0, 1)}
    dns_d = {L: din(f"dns{L}", [DN_SH, 44], F16) for L in (0, 1)}
    y_d = nc.dram_tensor("y", [D, TPC], F32, kind="ExternalOutput").ap()

    sh_A = nc.dram_tensor("sh_A", [2, 128, 16], F32, addr_space="Shared").ap()
    sh_x = nc.dram_tensor("sh_x", [2, D, TPC], BF16, addr_space="Shared").ap()
    bar_in = nc.dram_tensor("bar_in", [1, 16], F32).ap()
    bar_out = nc.dram_tensor("bar_out", [1, 16], F32).ap()
    aguA_in = {L: nc.dram_tensor(f"aguA_in{L}", [D, 768], BF16).ap() for L in (0, 1)}
    aguA_out = {L: nc.dram_tensor(f"aguA_out{L}", [8, D, 768], BF16).ap() for L in (0, 1)}
    aguB_in = {L: nc.dram_tensor(f"aguB_in{L}", [D, 640], BF16).ap() for L in (0, 1)}
    aguB_out = {L: nc.dram_tensor(f"aguB_out{L}", [8, D, 640], BF16).ap() for L in (0, 1)}
    adn_in = {L: nc.dram_tensor(f"adn_in{L}", [IFF, DN_SH], BF16).ap() for L in (0, 1)}
    adn_out = {L: nc.dram_tensor(f"adn_out{L}", [8, IFF, DN_SH], BF16).ap() for L in (0, 1)}
    spark_d = nc.dram_tensor("spark", [D, TPC], F32).ap()
    k_park_d = nc.dram_tensor("k_park", [NH, 128, S], BF16).ap()
    v_park_d = nc.dram_tensor("v_park", [S, D], BF16).ap()
    q_park_d = nc.dram_tensor("q_park", [NH, 128, TPC], BF16).ap()
    qr_park_d = nc.dram_tensor("qr_park", [NH, 64, TPC], BF16).ap()

    es = ExitStack()
    kb = KB(nc, es)

    def sb(name, shape, dt, st=None):
        return (st or es).enter_context(nc.sbuf_tensor(name, shape, dt))

    xb = sb("xb", [128, NCH * 1024], BF16)
    biases = sb("biases_sb", [128, 131], F32)
    ident = sb("ident_sb", [128, 128], F32)
    identb = sb("identb_sb", [128, 128], BF16)
    rmat = sb("rmat_sb", [128, 128], BF16)
    ones_c = sb("ones_c", [128, 1], BF16)
    ones_rf = sb("ones_rf", [1, 128], F32)
    ones_rb = sb("ones_rb", [1, 128], BF16)
    slot_sb = sb("slot_sb", [1, 1], I32)
    A_sb = sb("A_sb", [128, 16], F32)
    A_sb2 = sb("A_sb2", [128, 16], F32)
    rows = sb("rows", [1, 2048], F32)
    bca = sb("bca", [128, 1024], F32)
    bcb = sb("bcb", [128, 1024], F32)
    sq_sc = sb("sq_sc", [128, 1024], BF16)
    dq_pk = sb("dq_pk", [128, 704], I32)
    dq_sc = sb("dq_sc", [128, 44], F16)
    dq_t1 = sb("dq_t1", [128, 1024], I32)
    dq_t2 = sb("dq_t2", [128, 1024], I32)
    dq_u = sb("dq_u", [128, 1024], U16)
    dq_sx = sb("dq_sx", [128, 1024], F16)
    dq_w = sb("dq_w", [128, 1024], BF16)
    dq_tr = sb("dq_tr", [128, 1024], BF16)
    pb = [es.enter_context(nc.psum_tensor(f"pb{i}", [128, 512], F32)) for i in range(8)]

    def pbb(i):
        return pb[i][:].bitcast(BF16)

    def row(i, n=1024, off=0):
        return rows[0:1, i * 1024 + off:i * 1024 + off + n]

    def rowx(n=1024):
        return bca[0:1, 0:n]

    G.memset(ones_c[:], 1.0)
    G.memset(ones_rf[:], 1.0)
    ones_mk = kb.bump("gS", G.memset(ones_rb[:], 1.0))
    ld_biases = kb.dma(SY, "qS", biases[:], biases_d[:])
    ld_ident = kb.dma(SY, "qS", ident[:], ident_d[:])
    ld_identb = kb.dma(SY, "qS", identb[:], identb_d[:])
    ld_rmat = kb.dma(SY, "qS", rmat[:], rmat_d[:])
    ld_init = kb.dma(SY, "qS", slot_sb[:], slot_d[:])
    ld0 = ld_biases

    kb.wait(G, ld_init)
    rA = G.alloc_register("rAoff")
    rX = G.alloc_register("rXoff")
    G.reg_load(rA, slot_sb[0:1, 0:1])
    G.reg_load(rX, slot_sb[0:1, 0:1])
    G.reg_mul(rA, rA, 128 * 16)
    G.reg_mul(rX, rX, D * TPC)

    def vchain(inst):
        mk = kb.bump("vS", inst)
        kb.wait(V, mk)
        return mk

    def schain(inst):
        mk = kb.bump("sS", inst)
        kb.wait(SC, mk)
        return mk

    def mm_group(bank_ap, pairs, bias_k1=None):
        allp = ([bias_k1] if bias_k1 else []) + list(pairs)
        inst = None
        for i, (lh, rh) in enumerate(allp):
            inst = T.matmul(bank_ap, lh, rh, start=(i == 0), stop=(i == len(allp) - 1))
        return inst

    def bcast_row(src_row_f32, dst_sb, wait_mk, banks=(6, 7), n=1024):
        for half in range(2 if n > 512 else 1):
            w = min(512, n - half * 512)
            bk = banks[half]
            kb.use_bank(bk)
            kb.wait(T, wait_mk)
            tmk = kb.bump("tS", T.matmul(
                pb[bk][:, 0:w], ones_rf[:], src_row_f32[0:1, half * 512:half * 512 + w],
                start=True, stop=True))
            kb.wait(V, tmk)
            emk = kb.bump("vS", V.tensor_copy(out=dst_sb[:, half * 512:half * 512 + w],
                                              in_=pb[bk][:, 0:w]))
            kb.evac_mark(bk, emk)
        return kb.mark("vS")

    def sq_reduce(chunk_src, n_ch, dst_row, banks=(6, 7)):
        lred = None
        for c in range(n_ch):
            ch, wm = chunk_src(c)
            kb.wait(SC, wm)
            smk = kb.bump("sS", SC.activation(out=sq_sc[:], in_=ch, func=AF.Square))
            for half in range(2):
                bk = banks[half]
                if c == 0:
                    kb.use_bank(bk)
                kb.wait(T, smk)
                inst = T.matmul(pb[bk][0:1, 0:512], ones_c[:],
                                sq_sc[:, half * 512:half * 512 + 512],
                                start=(c == 0), stop=(c == n_ch - 1))
            lred = kb.bump("tS", inst)
            kb.wait(SC, lred)
        kb.wait(V, lred)
        kb.bump("vS", V.tensor_copy(out=dst_row[0:1, 0:512], in_=pb[banks[0]][0:1, 0:512]))
        mk = kb.bump("vS", V.tensor_copy(out=dst_row[0:1, 512:1024],
                                         in_=pb[banks[1]][0:1, 0:512]))
        kb.evac_mark(banks[0], mk)
        kb.evac_mark(banks[1], mk)
        return mk

    tapped = [False]

    def tap_store(src_ap_fn, nch=NCH, width=1024):
        kb.wait(G, kb.mark("vS"))
        kb.wait(G, kb.mark("sS"))
        kb.wait(G, kb.mark("tS"))
        last = None
        for c in range(nch):
            last = kb.dma(G, "qG", y_d[c * 128:(c + 1) * 128, 0:width], src_ap_fn(c))
        kb.wait(G, last)
        tapped[0] = True

    st_main = ExitStack()
    stream = sb("stream0", [128, NCH * 1024], F32, st_main)

    st0 = ExitStack()
    x_tm = sb("x_tm", [128, 8 * D], F32, st0)
    xld = [kb.dma(SY, "qS", x_tm[:, j * D:(j + 1) * D], x_d[j * 128:(j + 1) * 128, :])
           for j in range(NT)]
    kb.wait(SC, ld_biases)
    kb.wait(T, ones_mk)
    for j in range(NT):
        kb.wait(T, xld[j])
        if j == 0:
            kb.wait(T, ld_ident)
        for i in range(NCH):
            bk = (j * NCH + i) % 2
            kb.use_bank(bk)
            tmk = kb.bump("tS", T.transpose(
                pb[bk][:, 0:128], x_tm[:, j * D + i * 128:j * D + (i + 1) * 128], ident[:]))
            kb.wait(SC, tmk)
            emk = kb.bump("sS", SC.activation(
                out=stream[:, i * 1024 + j * 128:i * 1024 + (j + 1) * 128],
                in_=pb[bk][:, 0:128], func=AF.Copy))
            kb.evac_mark(bk, emk)
    x_trans_done = kb.mark("tS")
    kb.wait(SC, kb.mark("sS"))
    for i in range(NCH):
        kb.bump("sS", SC.activation(out=xb[:, i * 1024:(i + 1) * 1024],
                                    in_=stream[:, i * 1024:(i + 1) * 1024], func=AF.Copy))
    xb_ready = kb.mark("sS")
    st0.close()

    dq_state = {}

    def dequant(L, which, chunk_range, guard=None):
        if which == "gu":
            packed_d, scales_d = gup_d[L], gus_d[L]
            nwords, ng_ch, icols, uw = 256, 16, D, 128
        else:
            packed_d, scales_d = dnp_d[L], dns_d[L]
            nwords, ng_ch, icols, uw = 704, 44, IFF, 64
        ucols = uw * 8
        nunit = nwords // uw
        ships = []
        prev_ship = dq_state.get("ship")
        prev_add = dq_state.get("add")
        prev_tr = dq_state.get("tr")
        for ch in chunk_range:
            lp = kb.dma(G, "qG", dq_pk[:, 0:nwords],
                        packed_d[ch * 128:(ch + 1) * 128, :])
            lsc = kb.dma(G, "qG", dq_sc[:, 0:ng_ch],
                         scales_d[ch * 128:(ch + 1) * 128, :])
            if guard is not None:
                kb.wait(V, guard)
                guard = None
            for u in range(nunit):
                w = dq_pk[:, u * uw:(u + 1) * uw]
                kb.wait(V, lp)
                kb.wait(V, lsc)
                kb.wait(V, prev_ship)
                kb.wait(V, prev_add)
                sh_mk = None
                for k in range(8):
                    s1 = 9 - 4 * k
                    V.tensor_scalar(out=dq_t1[:, k * uw:(k + 1) * uw], in0=w,
                                    scalar1=abs(s1), scalar2=0x0E00,
                                    op0=ALU.logical_shift_left if s1 >= 0 else ALU.logical_shift_right,
                                    op1=ALU.bitwise_and)
                    s2 = 12 - 4 * k
                    inst = V.tensor_scalar(out=dq_t2[:, k * uw:(k + 1) * uw], in0=w,
                                           scalar1=abs(s2), scalar2=0x8000,
                                           op0=ALU.logical_shift_left if s2 >= 0 else ALU.logical_shift_right,
                                           op1=ALU.bitwise_and)
                sh_mk = kb.bump("vS", inst)
                kb.wait(V, sh_mk)
                for k in range(8):
                    inst = V.tensor_tensor(out=bass.AP(dq_u, k, [[1024, 128], [8, uw]]),
                                           in0=dq_t1[:, k * uw:(k + 1) * uw],
                                           in1=dq_t2[:, k * uw:(k + 1) * uw], op=ALU.add)
                prev_add = kb.bump("vS", inst)
                ngu = ucols // 128
                scv = bass.AP(dq_sc, u * ngu, [[44, 128], [1, ngu], [0, 128]])
                sx_mk = kb.bump("vS", V.tensor_scalar(
                    out=dq_sx[:, 0:ucols].rearrange("p (g r) -> p g r", r=128),
                    in0=scv, scalar1=16384.0, scalar2=None, op0=ALU.mult))
                kb.wait(V, sx_mk)
                kb.wait(V, prev_tr)
                vm = kb.bump("vS", V.tensor_tensor(
                    out=dq_w[:, 0:ucols], in0=dq_u[:, 0:ucols].bitcast(F16),
                    in1=dq_sx[:, 0:ucols], op=ALU.mult))
                kb.wait(SC, prev_ship)
                for it in range(ucols // 128):
                    bk = 2 + it % 2
                    kb.use_bank(bk)
                    kb.wait(T, vm)
                    kb.wait(T, ld_identb)
                    tmk = kb.bump("tS", T.transpose(
                        pbb(bk)[:, 0:128], dq_w[:, it * 128:(it + 1) * 128], identb[:]))
                    kb.wait(SC, tmk)
                    emk = kb.bump("sS", SC.activation(
                        out=dq_tr[:, it * 128:(it + 1) * 128], in_=pbb(bk)[:, 0:128],
                        func=AF.Copy))
                    kb.evac_mark(bk, emk)
                prev_tr = tmk
                kb.wait(G, kb.mark("sS"))
                if which == "gu":
                    if ch < 6:
                        dd, col0, ocols = aguA_in[L], ch * 128, 768
                    else:
                        dd, col0, ocols = aguB_in[L], (ch - 6) * 128, 640
                else:
                    dd, col0, ocols = adn_in[L], ch * 128, DN_SH
                dst = bass.AP(dd.tensor, dd.offset + (u * ucols) * ocols + col0,
                              [[ocols, 128], [128 * ocols, ucols // 128], [1, 128]])
                prev_ship = kb.dma(G, "qG", dst, dq_tr[:, 0:ucols])
                ships.append(prev_ship)
        dq_state.update(ship=prev_ship, add=prev_add, tr=prev_tr)
        return ships

    if lvl >= 2:
        gu0_shipA = dequant(0, "gu", range(0, 6), guard=x_trans_done)
        gu0_shipB = dequant(0, "gu", range(6, 11))
        dn0_ship = dequant(0, "dn", range(0, 2))

    st_w = [ExitStack()]

    wctr = [0]

    def open_wstat(dbl):
        st_w[0] = ExitStack()
        wctr[0] += 1
        n = wctr[0]
        w32a = sb(f"wstat32a_{n}", [128, D], F32, st_w[0])
        w32b = sb(f"wstat32b_{n}", [128, D], F32, st_w[0]) if dbl else w32a
        return ([w32a, w32b],
                [sb(f"wstat_a{n}", [128, D], BF16, st_w[0]),
                 sb(f"wstat_b{n}", [128, D], BF16, st_w[0])])

    wstate = {}

    def load_stat(buf, src_dram, col0, ncols, row_len, nch=NCH):
        w32, wst = wstate["w32"][buf], wstate["wst"]
        ap_src = bass.AP(src_dram.tensor, src_dram.offset + col0,
                         [[row_len, 128], [128 * row_len, nch], [1, ncols]])
        kb.wait(SY, wstate["cast32"][buf])
        if not wstate["dbl"]:
            kb.wait(SY, wstate["cast32"][1 - buf])
        lm = kb.dma(SY, "qS", w32[:, 0:nch * ncols], ap_src)
        kb.wait(V, lm)
        if wstate["busy"][buf] is not None:
            kb.wait(V, wstate["busy"][buf])
            wstate["busy"][buf] = None
        mk = kb.bump("vS", V.tensor_copy(out=wst[buf][:, 0:nch * ncols],
                                         in_=w32[:, 0:nch * ncols]))
        wstate["cast32"][buf] = mk
        return mk

    def begin_wphase(guard=None, dbl=True):
        w32, wst = open_wstat(dbl)
        wstate.update(w32=w32, wst=wst, busy=[None, None], cast32=[guard, guard],
                      dbl=dbl)

    def end_wphase():
        st_w[0].close()

    def proj16(e_idx, dst_ap, bias_col, src_dram, col0, row_len, mov, mov_wait,
               banks=(4, 5), nch=NCH, evac_f=None):
        buf = e_idx % 2
        wst = wstate["wst"]
        wm = load_stat(buf, src_dram, col0, 128, row_len, nch)
        kb.wait(T, wm)
        kb.wait(T, mov_wait)
        last = None
        for half in range(2):
            bk = banks[half]
            kb.use_bank(bk)
            pairs = [(wst[buf][:, i * 128:(i + 1) * 128], mov(i, half))
                     for i in range(nch)]
            tmk = kb.bump("tS", mm_group(pb[bk][:, 0:512], pairs))
            if half == 1:
                wstate["busy"][buf] = tmk
            if evac_f is not None:
                last = evac_f(half, bk, tmk)
            else:
                kb.wait(SC, tmk)
                last = kb.bump("sS", SC.activation(
                    out=dst_ap[:, half * 512:half * 512 + 512], in_=pb[bk][:, 0:512],
                    func=AF.Identity, bias=biases[:, bias_col:bias_col + 1]))
                kb.evac_mark(bk, last)
        return last

    def layernorm(stream_t, res_done_v):
        kb.wait(SC, res_done_v)
        zcp = []
        for i in range(NCH):
            zcp.append(kb.bump("sS", SC.activation(
                out=xb[:, i * 1024:(i + 1) * 1024],
                in_=stream_t[:, i * 1024:(i + 1) * 1024], func=AF.Copy)))
        for half in range(2):
            bk = (6, 7)[half]
            kb.use_bank(bk)
            for i in range(NCH):
                if half == 0:
                    kb.wait(T, zcp[i])
                inst = T.matmul(pb[bk][0:1, 0:512], ones_c[:],
                                xb[:, i * 1024 + half * 512:i * 1024 + half * 512 + 512],
                                start=(i == 0), stop=(i == NCH - 1))
            tmk = kb.bump("tS", inst)
            kb.wait(V, tmk)
            emk = kb.bump("vS", V.tensor_copy(out=row(0, 512, half * 512),
                                              in_=pb[bk][0:1, 0:512]))
            kb.evac_mark(bk, emk)
        sum_mk = kb.mark("vS")
        ssq_mk = sq_reduce(lambda c: (xb[:, c * 1024:(c + 1) * 1024], zcp[c]), NCH, row(1))
        kb.wait(V, sum_mk)
        kb.wait(V, ssq_mk)
        vchain(V.tensor_scalar(out=row(0), in0=row(0), scalar1=1.0 / D, scalar2=None,
                               op0=ALU.mult))
        vchain(V.tensor_scalar(out=row(1), in0=row(1), scalar1=1.0 / D, scalar2=None,
                               op0=ALU.mult))
        vchain(V.tensor_tensor(out=rowx(), in0=row(0), in1=row(0), op=ALU.mult))
        var_mk = kb.bump("vS", V.tensor_tensor(out=row(1), in0=row(1), in1=rowx(),
                                               op=ALU.subtract))
        kb.wait(SC, var_mk)
        sdm = kb.bump("sS", SC.activation(out=row(1), in_=row(1), func=AF.Sqrt, bias=biases[0:1, 129:130]))
        kb.wait(V, sdm)
        rstd_mk = vchain(V.reciprocal(out=row(1), in_=row(1)))
        vchain(V.tensor_tensor(out=row(0), in0=row(0), in1=row(1), op=ALU.mult))
        brow_mk = kb.bump("vS", V.tensor_scalar(out=row(0), in0=row(0), scalar1=-1.0,
                                                scalar2=None, op0=ALU.mult))
        a_bc = bcast_row(row(1), bca, rstd_mk)
        b_bc = bcast_row(row(0), bcb, brow_mk)
        last = None
        for c in range(NCH):
            if c == 0:
                kb.wait(V, a_bc)
                kb.wait(V, b_bc)
            vchain(V.tensor_tensor(out=stream_t[:, c * 1024:(c + 1) * 1024],
                                   in0=stream_t[:, c * 1024:(c + 1) * 1024], in1=bca[:],
                                   op=ALU.mult))
            vchain(V.tensor_tensor(out=stream_t[:, c * 1024:(c + 1) * 1024],
                                   in0=stream_t[:, c * 1024:(c + 1) * 1024], in1=bcb[:],
                                   op=ALU.add))
            last = kb.bump("vS", V.tensor_copy(out=xb[:, c * 1024:(c + 1) * 1024],
                                               in_=stream_t[:, c * 1024:(c + 1) * 1024]))
        return last

    def park_stream(stream_t, ready_v):
        kb.wait(SY, ready_v)
        return [kb.dma(SY, "qS", spark_d[c * 128:(c + 1) * 128, :],
                       stream_t[:, c * 1024:(c + 1) * 1024]) for c in range(NCH)]

    def reload_stream(stream_t):
        return [kb.dma(SY, "qS", stream_t[:, c * 1024:(c + 1) * 1024],
                       spark_d[c * 128:(c + 1) * 128, :]) for c in range(NCH)]

    def wait_all(eng, mks):
        seen = {}
        for mk in mks:
            if mk is not None:
                seen[mk[0]] = max(seen.get(mk[0], 0), mk[1])
        for name, v in seen.items():
            eng.wait_ge(kb.sems[name], v)

    a_ship = None
    if lvl >= 1:
        begin_wphase(x_trans_done)
        st1 = ExitStack()
        qkv = sb("qkv", [128, 16 * 1024], BF16, st1)
        vtmp = sb("vtmp", [128, 2 * 1024], BF16, st1)
        tabs = sb("tabs", [128, 2048], BF16, st1)
        rtmp = sb("rtmp", [128, 1024], BF16, st1)
        rtmp2 = sb("rtmp2", [128, 1024], BF16, st1)
        ap_part = sb("ap_part", [128, 16], F32, st1)

        def mov_xb(i, half):
            return xb[:, i * 1024 + half * 512:i * 1024 + half * 512 + 512]

        k_mks = [proj16(e, qkv[:, (e - 16) * 1024:(e - 15) * 1024], e,
                        w_inT_d, e * 128, 3 * D, mov_xb,
                        xb_ready if e == 16 else None)
                 for e in range(16, 32)]

        def rope8(chunk_ap, cosd, sind, chunk_ready, bank=6, guard0=None):
            done = []
            tl_prev = guard0
            for cch in range(8):
                kb.wait(SY, tl_prev)
                tl1 = kb.dma(SY, "qS", tabs[:, 0:1024], cosd[cch * 128:(cch + 1) * 128, :])
                tl2 = kb.dma(SY, "qS", tabs[:, 1024:2048], sind[cch * 128:(cch + 1) * 128, :])
                ch = chunk_ap(cch)
                kb.wait(T, chunk_ready[cch])
                kb.wait(T, ld_rmat)
                for half in range(2):
                    kb.use_bank(bank)
                    tmk = kb.bump("tS", T.matmul(
                        pb[bank][:, 0:512], rmat[:], ch[:, half * 512:half * 512 + 512],
                        start=True, stop=True))
                    kb.wait(SC, tmk)
                    kb.wait(SC, tl_prev)
                    emk = kb.bump("sS", SC.activation(
                        out=rtmp[:, half * 512:half * 512 + 512], in_=pb[bank][:, 0:512],
                        func=AF.Copy))
                    kb.evac_mark(bank, emk)
                kb.wait(V, kb.mark("sS"))
                kb.wait(V, tl1)
                kb.wait(V, tl2)
                V.tensor_tensor(out=rtmp[:], in0=rtmp[:], in1=tabs[:, 1024:2048], op=ALU.mult)
                r2 = kb.bump("vS", V.tensor_tensor(out=rtmp2[:], in0=ch,
                                                   in1=tabs[:, 0:1024], op=ALU.mult))
                kb.wait(V, r2)
                dm = kb.bump("vS", V.tensor_tensor(out=ch, in0=rtmp[:], in1=rtmp2[:],
                                                   op=ALU.add))
                tl_prev = dm
                done.append(dm)
            return done

        if tap == "kraw":
            tap_store(lambda c: qkv[:, c * 1024:(c + 1) * 1024])
            st1.close(); end_wphase(); st_main.close(); es.close()
            return nc
        ropeK = rope8(lambda c: qkv[:, c * 1024:(c + 1) * 1024], cos0_d, sin0_d, k_mks)
        if tap == "k":
            tap_store(lambda c: qkv[:, c * 1024:(c + 1) * 1024])
            st1.close(); end_wphase(); st_main.close(); es.close()
            return nc

        def ksrc(c):
            return (qkv[:, c * 1024:(c + 1) * 1024], ropeK[c] if c < 8 else k_mks[c])

        nk = sq_reduce(ksrc, 16, row(0))
        kb.wait(SC, nk)
        sqm = kb.bump("sS", SC.activation(out=row(1), in_=row(0), func=AF.Sqrt))
        kb.wait(V, sqm)
        vchain(V.tensor_scalar(out=row(1), in0=row(1), scalar1=1e-12, scalar2=None,
                               op0=ALU.max))
        rn_mk = kb.bump("vS", V.reciprocal(out=row(1), in_=row(1)))
        kb.wait(V, rn_mk)
        rn_bc = bcast_row(row(1), bca, rn_mk)
        if tap == "rnorm":
            kb.wait(SY, rn_bc)
            tap_store(lambda c: bca[:, 0:1024], nch=1)
            st1.close(); end_wphase(); st_main.close(); es.close()
            return nc

        pa_mks = []
        vconsume = [None, None]
        for c in range(16):
            def evac_v(half, bk, tmk, c=c):
                kb.wait(SC, tmk)
                emk = kb.bump("sS", SC.activation(
                    out=vtmp[:, (c % 2) * 1024 + half * 512:(c % 2) * 1024 + half * 512 + 512],
                    in_=pb[bk][:, 0:512], func=AF.Identity, bias=biases[:, 32 + c:33 + c]))
                kb.evac_mark(bk, emk)
                return emk
            vev = proj16(32 + c, None, None, w_inT_d, (32 + c) * 128, 3 * D, mov_xb,
                         vconsume[c % 2], evac_f=evac_v)
            kb.wait(V, vev)
            if c == 0:
                kb.wait(V, rn_bc)
            if pa_mks:
                kb.wait(V, pa_mks[-1])
            vchain(V.tensor_tensor(out=rtmp[:], in0=qkv[:, c * 1024:(c + 1) * 1024],
                                   in1=vtmp[:, (c % 2) * 1024:(c % 2 + 1) * 1024],
                                   op=ALU.mult))
            vchain(V.tensor_tensor(out=rtmp[:], in0=rtmp[:], in1=bca[:], op=ALU.mult))
            pa = kb.bump("vS", V.tensor_reduce(
                out=ap_part[:, c:c + 1], in_=rtmp[:], axis=AX.X, op=ALU.add))
            pa_mks.append(pa)
            vconsume[c % 2] = pa
        kb.wait(G, pa_mks[-1])
        a_ship = kb.dma(G, "qG", bass.AP(sh_A.tensor, rA, [[16, 128], [1, 16]]),
                        ap_part[:])

        q_mks = [proj16(e, qkv[:, e * 1024:(e + 1) * 1024], e,
                        w_inT_d, e * 128, 3 * D, mov_xb,
                        pa_mks[-1] if e == 0 else None)
                 for e in range(16)]
        if tap == "q3":
            kb.wait(G, q_mks[-1])
            for i in range(16):
                mm = kb.dma(G, "qG", y_d[i * 128:(i + 1) * 128, :],
                            qkv[:, i * 1024:(i + 1) * 1024])
            kb.wait(G, mm)
            st1.close(); end_wphase(); st_main.close(); es.close()
            return nc
        ropeQ = rope8(lambda c: qkv[:, c * 1024:(c + 1) * 1024], cos0_d, sin0_d, q_mks,
                      guard0=pa_mks[-1])

    ccm = {}
    if lvl >= 2:
        kb.wait(G, gu0_shipA[-1])
        ccm["agA0"] = kb.bump("cc", G.collective_compute(
            "AllGather", ALU.bypass, replica_groups=ALL8,
            ins=[aguA_in[0][:]], outs=[aguA_out[0][:]]))
    if lvl >= 1:
        kb.wait(G, a_ship)
        bi = kb.dma(G, "qG", bar_in[:], biases[0:1, 0:16])
        kb.wait(G, bi)
        ccm["bar1"] = kb.bump("cc", G.collective_compute(
            "AllReduce", ALU.add, replica_groups=PAIRS,
            ins=[bar_in[:]], outs=[bar_out[:]]))
    if lvl >= 2:
        kb.wait(G, gu0_shipB[-1])
        ccm["agB0"] = kb.bump("cc", G.collective_compute(
            "AllGather", ALU.bypass, replica_groups=ALL8,
            ins=[aguB_in[0][:]], outs=[aguB_out[0][:]]))
        kb.wait(G, dn0_ship[-1])
        ccm["agD0"] = kb.bump("cc", G.collective_compute(
            "AllGather", ALU.bypass, replica_groups=ALL8,
            ins=[adn_in[0][:]], outs=[adn_out[0][:]]))

    if tap == "wgu" and lvl >= 2:
        kb.wait(G, ("cc", kb.c["cc"]))
        for i in range(16):
            m1 = kb.dma(G, "qG", dq_w[:, 0:768],
                        aguA_out[0][0][i * 128:(i + 1) * 128, :])
            kb.wait(G, m1)
            m2 = kb.dma(G, "qG", y_d[i * 128:(i + 1) * 128, 0:768], dq_w[:, 0:768])
            kb.wait(G, m2)
        for stx in (st1, ):
            stx.close()
        end_wphase()
        st_main.close()
        es.close()
        return nc

    if lvl >= 1:
        kb.wait(SY, ccm["bar1"])
        a0 = kb.dma(SY, "qS", A_sb[:], sh_A[0])
        a1 = kb.dma(SY, "qS", A_sb2[:], sh_A[1])
        kb.wait(V, a0)
        kb.wait(V, a1)
        A_mk = kb.bump("vS", V.tensor_tensor(out=A_sb[:], in0=A_sb[:], in1=A_sb2[:],
                                             op=ALU.add))
        if tap == "A":
            kb.wait(SY, A_mk)
            tap_store(lambda c: A_sb[:, 0:16], nch=1, width=16)
            st1.close(); end_wphase(); st_main.close(); es.close()
            return nc
        y_mks = []
        for c in range(16):
            if c == 0:
                kb.wait(V, A_mk)
            kb.wait(V, ropeQ[c] if c < 8 else q_mks[c])
            y_mks.append(kb.bump("vS", V.tensor_scalar(
                out=qkv[:, c * 1024:(c + 1) * 1024], in0=qkv[:, c * 1024:(c + 1) * 1024],
                scalar1=A_sb[:, c:c + 1], scalar2=None, op0=ALU.mult)))

        if tap == "q2":
            kb.wait(G, y_mks[-1])
            for i in range(16):
                mm = kb.dma(G, "qG", y_d[i * 128:(i + 1) * 128, :],
                            qkv[:, i * 1024:(i + 1) * 1024])
            kb.wait(G, mm)
            st1.close(); end_wphase(); st_main.close(); es.close()
            return nc

        def mov_yy(i, half):
            return qkv[:, i * 1024 + half * 512:i * 1024 + half * 512 + 512]

        dbl_mk = None
        for c in range(NCH):
            dbl_mk = kb.bump("vS", V.tensor_tensor(
                out=stream[:, c * 1024:(c + 1) * 1024],
                in0=stream[:, c * 1024:(c + 1) * 1024],
                in1=stream[:, c * 1024:(c + 1) * 1024], op=ALU.add))
        kb.wait(V, dbl_mk)

        res_last = None
        for e in range(16):
            def evac_res(half, bk, tmk, e=e):
                kb.wait(V, tmk)
                emk = kb.bump("vS", V.scalar_tensor_tensor(
                    out=stream[:, e * 1024 + half * 512:e * 1024 + half * 512 + 512],
                    in0=pb[bk][:, 0:512], scalar=biases[:, 48 + e:49 + e],
                    in1=stream[:, e * 1024 + half * 512:e * 1024 + half * 512 + 512],
                    op0=ALU.add, op1=ALU.add))
                kb.evac_mark(bk, emk)
                return emk
            res_last = proj16(e, None, None, w_outT_d, e * 128, D, mov_yy,
                              y_mks[-1] if e == 0 else None, evac_f=evac_res)
        if tap == "q":
            kb.wait(SY, y_mks[-1])
            tap_store(lambda c: qkv[:, c * 1024:(c + 1) * 1024])
            st1.close()
            end_wphase()
            st_main.close()
            es.close()
            return nc
        st1.close()
        end_wphase()
        if tap == "yattn":
            kb.wait(SY, res_last)
            tap_store(lambda c: stream[:, c * 1024:(c + 1) * 1024])
            st_main.close()
            es.close()
            return nc
        ln1 = layernorm(stream, res_last)
        park1 = park_stream(stream, ln1)
        st_main.close()

    def ffn(agA_t, agB_t, adn_t, in_mk, agA_mk, agB_mk, agD_mk, park_mk):
        fid = ffn.ctr
        stF = ExitStack()
        h = sb(f"h_ffn{fid}", [128, 44 * 1024], BF16, stF)
        stGU = ExitStack()
        wg = sb(f"wg_ffn{fid}", [128, NCH * 256], BF16, stGU)
        wu = sb(f"wu_ffn{fid}", [128, NCH * 256], BF16, stGU)
        sg = sb(f"sg_ffn{fid}", [128, 1024], BF16, stGU)
        ub = sb(f"ub_ffn{fid}", [128, 1024], BF16, stGU)
        wg_busy = [None]
        wu_busy = [None]
        first_guard = [park_mk]

        def stat_load2(dst, out_t, rank, j0, nj, busy):
            ocols = int(out_t.tensor.shape[2])
            kb.wait(SY, busy[0])
            if first_guard[0] is not None:
                wait_all(SY, first_guard[0])
                first_guard[0] = None
            src = bass.AP(out_t.tensor, out_t.offset + rank * D * ocols + j0 * 128,
                          [[ocols, 128], [128 * ocols, NCH], [1, nj * 128]])
            return kb.dma(SY, "qS", dst[:, 0:NCH * nj * 128], src)

        hm = None
        for part, agt, agmk in (("A", agA_t, agA_mk), ("B", agB_t, agB_mk)):
            jlist = [(0, 2), (2, 2), (4, 2)] if part == "A" else [(0, 2), (2, 2), (4, 1)]
            for r in range(4):
                for (j0, nj) in jlist:
                    if r == 0 and j0 == 0:
                        kb.wait(SY, agmk)
                    gm = stat_load2(wg, agt, r, j0, nj, wg_busy)
                    um = stat_load2(wu, agt, r + 4, j0, nj, wu_busy)
                    for sub in range(nj):
                        hidx = 11 * r + (j0 + sub if part == "A" else 6 + j0 + sub)
                        mk_pairs = {}
                        for which, wsb, wm, bks in (("g", wg, gm, (0, 1)),
                                                    ("u", wu, um, (4, 5))):
                            tmks = []
                            for half in range(2):
                                bk = bks[half]
                                kb.use_bank(bk)
                                kb.wait(T, wm)
                                if in_mk is not None:
                                    kb.wait(T, in_mk)
                                prs = [(wsb[:, i * nj * 128 + sub * 128:i * nj * 128 + sub * 128 + 128],
                                        xb[:, i * 1024 + half * 512:i * 1024 + half * 512 + 512])
                                       for i in range(NCH)]
                                tmks.append(kb.bump("tS", mm_group(pb[bk][:, 0:512], prs)))
                            mk_pairs[which] = tmks
                        in_mk = None
                        wg_busy[0] = mk_pairs["g"][-1]
                        wu_busy[0] = mk_pairs["u"][-1]
                        kb.wait(SC, hm)
                        for half in range(2):
                            kb.wait(SC, mk_pairs["g"][half])
                            m1 = kb.bump("sS", SC.activation(
                                out=sg[:, half * 512:half * 512 + 512],
                                in_=pb[(0, 1)[half]][:, 0:512], func=AF.Silu))
                            kb.evac_mark((0, 1)[half], m1)
                            kb.wait(SC, mk_pairs["u"][half])
                            m2 = kb.bump("sS", SC.activation(
                                out=ub[:, half * 512:half * 512 + 512],
                                in_=pb[(4, 5)[half]][:, 0:512], func=AF.Copy))
                            kb.evac_mark((4, 5)[half], m2)
                        kb.wait(V, kb.mark("sS"))
                        hm = kb.bump("vS", V.tensor_tensor(
                            out=h[:, hidx * 1024:(hidx + 1) * 1024], in0=sg[:], in1=ub[:],
                            op=ALU.mult))
        if tap and tap.startswith("h"):
            base = int(tap[1:] or 0)
            kb.wait(G, hm)
            for i in range(16):
                hc = min(base + i, 43)
                mm = kb.dma(G, "qG", y_d[i * 128:(i + 1) * 128, :],
                            h[:, hc * 1024:(hc + 1) * 1024])
            kb.wait(G, mm)
            tapped[0] = True
            stGU.close()
            stF.close()
            return None, None, None
        stGU.close()
        stD = ExitStack()
        wd = sb(f"wd_ffn{fid}", [128, 44 * 256], BF16, stD)
        wd_busy = [None]
        wd_guard = [kb.mark("vS")]
        y_done = None
        for e2 in range(8):
            if e2 == 0:
                kb.wait(SY, agD_mk)
                kb.wait(SY, wd_guard[0])
            kb.wait(SY, wd_busy[0])
            src = bass.AP(adn_t.tensor, adn_t.offset + e2 * IFF * DN_SH,
                          [[DN_SH, 128], [128 * DN_SH, 44], [1, 256]])
            dml = kb.dma(SY, "qS", wd[:], src)
            for sub in range(2):
                oc = 2 * e2 + sub
                kb.wait(T, dml)
                if e2 == 0 and sub == 0:
                    kb.wait(T, hm)
                for half in range(2):
                    bk = (0, 1)[half]
                    kb.use_bank(bk)
                    prs = [(wd[:, i * 256 + sub * 128:i * 256 + sub * 128 + 128],
                            h[:, i * 1024 + half * 512:i * 1024 + half * 512 + 512])
                           for i in range(44)]
                    tmk = kb.bump("tS", mm_group(pb[bk][:, 0:512], prs))
                    if half == 1:
                        wd_busy[0] = tmk
                    kb.wait(SC, tmk)
                    emk = kb.bump("sS", SC.activation(
                        out=xb[:, oc * 1024 + half * 512:oc * 1024 + half * 512 + 512],
                        in_=pb[bk][:, 0:512], func=AF.Copy))
                    kb.evac_mark(bk, emk)
                    y_done = emk
        if tap == "yffn":
            kb.wait(G, y_done)
            kb.wait(G, ("sS", kb.c["sS"]))
            for i in range(16):
                mm = kb.dma(G, "qG", y_d[i * 128:(i + 1) * 128, :],
                            xb[:, i * 1024:(i + 1) * 1024])
            kb.wait(G, mm)
            tapped[0] = True
            stD.close()
            stF.close()
            return None, None, None
        stD.close()
        stF.close()
        st_f = ExitStack()
        stream_t = sb(f"stream_f{ffn.ctr}", [128, NCH * 1024], F32, st_f)
        ffn.ctr += 1
        ssq = sq_reduce(lambda c: (xb[:, c * 1024:(c + 1) * 1024],
                                   y_done if c == 0 else None), NCH, row(0))
        kb.wait(SY, y_done)
        rel = reload_stream(stream_t)
        kb.wait(V, ssq)
        vm = kb.bump("vS", V.tensor_scalar(out=row(0), in0=row(0), scalar1=1.0 / D,
                                           scalar2=None, op0=ALU.mult))
        kb.wait(SC, vm)
        sdm = kb.bump("sS", SC.activation(out=row(0), in_=row(0), func=AF.Sqrt, bias=biases[0:1, 130:131]))
        kb.wait(V, sdm)
        rr = vchain(V.reciprocal(out=row(0), in_=row(0)))
        r_bc = bcast_row(row(0), bca, rr)
        last = None
        for c in range(NCH):
            if c == 0:
                kb.wait(V, r_bc)
                wait_all(V, rel)
            vchain(V.tensor_tensor(out=sq_sc[:], in0=xb[:, c * 1024:(c + 1) * 1024],
                                   in1=bca[:], op=ALU.mult))
            last = kb.bump("vS", V.tensor_tensor(
                out=stream_t[:, c * 1024:(c + 1) * 1024],
                in0=stream_t[:, c * 1024:(c + 1) * 1024], in1=sq_sc[:], op=ALU.add))
            kb.wait(V, last)
        ln = layernorm(stream_t, last)
        return stream_t, ln, st_f

    ffn.ctr = 0

    if lvl >= 3:
        gu1_shipA = dequant(1, "gu", range(0, 6))
        gu1_shipB = dequant(1, "gu", range(6, 11))
        dn1_ship = dequant(1, "dn", range(0, 2))

    if lvl >= 2:
        stream2, ln2, st_f2 = ffn(aguA_out[0], aguB_out[0], adn_out[0], ln1,
                                  ccm["agA0"], ccm["agB0"], ccm["agD0"], park1)
        if tapped[0]:
            es.close()
            return nc
        final_stream, final_ln = stream2, ln2

    if lvl >= 3:
        kb.wait(G, ln2)
        xship = kb.dma(G, "qG", bass.AP(sh_x.tensor, rX,
                                        [[TPC, 128], [128 * TPC, NCH], [1, TPC]]),
                       xb[:])
        kb.wait(G, gu1_shipA[-1])
        ccm["agA1"] = kb.bump("cc", G.collective_compute(
            "AllGather", ALU.bypass, replica_groups=ALL8,
            ins=[aguA_in[1][:]], outs=[aguA_out[1][:]]))
        kb.wait(G, xship)
        bi2 = kb.dma(G, "qG", bar_in[:], biases[0:1, 0:16])
        kb.wait(G, bi2)
        ccm["bar2"] = kb.bump("cc", G.collective_compute(
            "AllReduce", ALU.add, replica_groups=PAIRS,
            ins=[bar_in[:]], outs=[bar_out[:]]))
        kb.wait(G, gu1_shipB[-1])
        ccm["agB1"] = kb.bump("cc", G.collective_compute(
            "AllGather", ALU.bypass, replica_groups=ALL8,
            ins=[aguB_in[1][:]], outs=[aguB_out[1][:]]))
        kb.wait(G, dn1_ship[-1])
        ccm["agD1"] = kb.bump("cc", G.collective_compute(
            "AllGather", ALU.bypass, replica_groups=ALL8,
            ins=[adn_in[1][:]], outs=[adn_out[1][:]]))

        park2 = park_stream(stream2, ln2)
        st_f2.close()

        st3 = ExitStack()
        masksb = sb("masksb", [128, 16 * 128], F32, st3)
        coskb = sb("coskb", [64, S], BF16, st3)
        sinkb = sb("sinkb", [64, S], BF16, st3)
        kRt = sb("kRt", [64, S], BF16, st3)
        mload = kb.dma(SY, "qS", masksb[:],
                       bass.AP(masks_d.tensor, masks_d.offset,
                               [[128, 128], [128 * 128, 16], [1, 128]]))
        ck_l = kb.dma(SY, "qS", coskb[:], cosk_d[:])
        sk_l = kb.dma(SY, "qS", sinkb[:], sink_d[:])
        wait_all(SY, park2)

        begin_wphase(None, dbl=False)
        st3q = ExitStack()
        cq = sb("cq", [128, 4 * 1024], BF16, st3q)
        qrb = sb("qrb", [128, 8 * 1024], BF16, st3q)
        tabs3 = sb("tabs3", [128, 2048], BF16, st3q)
        rtmp3 = sb("rtmp3", [128, 1024], BF16, st3q)
        rtmp4 = sb("rtmp4", [128, 1024], BF16, st3q)
        qtmp = sb("qtmp", [128, 1024], BF16, st3q)

        def mov_xb2(i, half):
            return xb[:, i * 1024 + half * 512:i * 1024 + half * 512 + 512]

        cq_mks = [proj16(l, cq[:, l * 1024:(l + 1) * 1024], 68 + l,
                         wdn_d, 512 + l * 128, 1024, mov_xb2,
                         ln2 if l == 0 else None)
                  for l in range(4)]

        def mov_cq(i, half):
            return cq[:, i * 1024 + half * 512:i * 1024 + half * 512 + 512]

        qship = []
        prev_q = None
        for h in range(NH):
            kb.wait(SY, prev_q)
            qm = proj16(h, qtmp, 88 + h, wuq_d, h * 128, D, mov_cq,
                        cq_mks[-1] if h == 0 else None, nch=4)
            kb.wait(SY, qm)
            prev_q = kb.dma(SY, "qS", q_park_d[h], qtmp[:])
            qship.append(prev_q)

        qr_mks = [proj16(c, qrb[:, c * 1024:(c + 1) * 1024], 104 + c,
                         wqr_d, c * 128, 1024, mov_xb2, None)
                  for c in range(8)]

        ropeQ3 = []
        tl_prev = None
        for cch in range(8):
            kb.wait(SY, tl_prev)
            tl1 = kb.dma(SY, "qS", tabs3[:, 0:1024], cosq_d[cch * 128:(cch + 1) * 128, :])
            tl2 = kb.dma(SY, "qS", tabs3[:, 1024:2048], sinq_d[cch * 128:(cch + 1) * 128, :])
            ch = qrb[:, cch * 1024:(cch + 1) * 1024]
            kb.wait(T, qr_mks[cch])
            kb.wait(T, ld_rmat)
            for half in range(2):
                kb.use_bank(6)
                tmk = kb.bump("tS", T.matmul(
                    pb[6][:, 0:512], rmat[:], ch[:, half * 512:half * 512 + 512],
                    start=True, stop=True))
                kb.wait(SC, tmk)
                kb.wait(SC, tl_prev)
                emk = kb.bump("sS", SC.activation(
                    out=rtmp3[:, half * 512:half * 512 + 512], in_=pb[6][:, 0:512],
                    func=AF.Copy))
                kb.evac_mark(6, emk)
            kb.wait(V, kb.mark("sS"))
            kb.wait(V, tl1)
            kb.wait(V, tl2)
            V.tensor_tensor(out=rtmp3[:], in0=rtmp3[:], in1=tabs3[:, 1024:2048], op=ALU.mult)
            r2 = kb.bump("vS", V.tensor_tensor(out=rtmp4[:], in0=ch,
                                               in1=tabs3[:, 0:1024], op=ALU.mult))
            kb.wait(V, r2)
            dm = kb.bump("vS", V.tensor_tensor(out=ch, in0=rtmp3[:], in1=rtmp4[:],
                                               op=ALU.add))
            tl_prev = dm
            ropeQ3.append(dm)
        qrship = []
        for c in range(8):
            kb.wait(SY, ropeQ3[c])
            qrship.append(kb.dma(SY, "qS", qr_park_d[2 * c],
                                 qrb[0:64, c * 1024:(c + 1) * 1024]))
            qrship.append(kb.dma(SY, "qS", qr_park_d[2 * c + 1],
                                 qrb[64:128, c * 1024:(c + 1) * 1024]))
        st3q.close()
        end_wphase()

        wait_all(SY, qship)
        wait_all(SY, qrship)
        st3kv = ExitStack()
        xg = [sb(f"xg{i}", [128, NCH * 256], BF16, st3kv) for i in range(2)]
        ckv = sb("ckv", [128, 4 * 256], BF16, st3kv)
        krt_s = sb("krt_s", [64, 256], BF16, st3kv)
        krh = sb("krh", [64, 256], BF16, st3kv)
        kst2 = [sb(f"kst{i}", [128, 256], BF16, st3kv) for i in range(2)]
        vst2 = [sb(f"vst{i}", [128, 2048], BF16, st3kv) for i in range(2)]
        wukv_v = sb("wukv_v", [128, 4 * 2048], BF16, st3kv)
        wukv_k = sb("wukv_k", [128, 4 * 2048], BF16, st3kv)
        wdn_r = sb("wdn_r", [128, 4 * 2048], BF16, st3kv)
        wkr_r = sb("wkr_r", [128, 16 * 64], BF16, st3kv)
        sq64 = sb("sq64", [64, 256], BF16, st3kv)
        vbias = sb("vbias_sb", [1, D], BF16, st3kv)
        wv32 = sb("wv32", [128, 2048], F32, st3kv)
        vb_l = kb.dma(SY, "qS", vbias[:], vbias_d[:])
        wvm = None
        res_loads = []
        for l in range(4):
            res_loads.append((wukv_v[:, l * 2048:(l + 1) * 2048],
                              bass.AP(wukv_d.tensor,
                                      wukv_d.offset + l * 128 * 4096 + 2048,
                                      [[4096, 128], [1, 2048]]), 2048))
        for l in range(4):
            res_loads.append((wukv_k[:, l * 2048:(l + 1) * 2048],
                              bass.AP(wukv_d.tensor, wukv_d.offset + l * 128 * 4096,
                                      [[4096, 128], [1, 2048]]), 2048))
        for l in range(4):
            res_loads.append((wdn_r[:, l * 2048:(l + 1) * 2048],
                              bass.AP(wdn_d.tensor, wdn_d.offset + l * 128,
                                      [[1024, 128], [128 * 1024, 16], [1, 128]]), 2048))
        res_loads.append((wkr_r[:, 0:1024],
                          bass.AP(wkr_d.tensor, wkr_d.offset,
                                  [[64, 128], [128 * 64, 16], [1, 64]]), 1024))
        for dst, src, w in res_loads:
            kb.wait(SY, wvm)
            lm = kb.dma(SY, "qS", wv32[:, 0:w], src)
            kb.wait(V, lm)
            kb.wait(V, wvm)
            wvm = kb.bump("vS", V.tensor_copy(out=dst, in_=wv32[:, 0:w]))
        wres_mk = wvm
        xbar = ccm["bar2"]
        kmax_mks = [None] * NH
        vship, kship = [], []
        xg_busy = [None, None]
        prev_v = [None, None]
        prev_kst = [None, None]
        rowKR_mk = None
        vev = None
        for g in range(8):
            buf = g % 2
            kb.wait(SY, xbar)
            kb.wait(SY, xg_busy[buf])
            xl = [kb.dma(SY, "qS", bass.AP(xg[buf], par * 128,
                                           [[NCH * 256, 128], [256, NCH], [1, 128]]),
                         bass.AP(sh_x.tensor, sh_x.offset + par * D * TPC + g * 128,
                                 [[TPC, 128], [128 * TPC, NCH], [1, 128]]))
                  for par in range(2)]
            ck_mks = []
            for l in range(4):
                kb.wait(T, wres_mk)
                kb.wait(T, xl[0])
                kb.wait(T, xl[1])
                bk = (0, 1)[l % 2]
                kb.use_bank(bk)
                prs = [(wdn_r[:, l * 2048 + i * 128:l * 2048 + (i + 1) * 128],
                        xg[buf][:, i * 256:(i + 1) * 256]) for i in range(NCH)]
                tmk = kb.bump("tS", mm_group(pb[bk][:, 0:256], prs))
                kb.wait(SC, tmk)
                kb.wait(SC, vev)
                emk = kb.bump("sS", SC.activation(
                    out=ckv[:, l * 256:(l + 1) * 256], in_=pb[bk][:, 0:256],
                    func=AF.Identity, bias=biases[:, 64 + l:65 + l]))
                kb.evac_mark(bk, emk)
                ck_mks.append(emk)
            kb.use_bank(2)
            prs = [(wkr_r[:, i * 64:(i + 1) * 64],
                    xg[buf][:, i * 256:(i + 1) * 256]) for i in range(NCH)]
            tmk = kb.bump("tS", mm_group(pb[2][0:64, 0:256], prs))
            kb.wait(SC, tmk)
            kr_mk = kb.bump("sS", SC.activation(
                out=krt_s[:], in_=pb[2][0:64, 0:256], func=AF.Identity,
                bias=biases[0:64, 128:129]))
            kb.evac_mark(2, kr_mk)
            kb.wait(T, kr_mk)
            kb.wait(T, ld_rmat)
            kb.use_bank(3)
            tmk = kb.bump("tS", T.matmul(pb[3][0:64, 0:256], rmat[0:64, 0:64],
                                         krt_s[:], start=True, stop=True))
            kb.wait(SC, tmk)
            rh_mk = kb.bump("sS", SC.activation(out=krh[:], in_=pb[3][0:64, 0:256],
                                                func=AF.Copy))
            kb.evac_mark(3, rh_mk)
            kb.wait(V, rh_mk)
            kb.wait(V, ck_l)
            kb.wait(V, sk_l)
            vchain(V.tensor_tensor(out=krh[:], in0=krh[:],
                                   in1=sinkb[:, g * 256:(g + 1) * 256], op=ALU.mult))
            vchain(V.tensor_tensor(out=krt_s[:], in0=krt_s[:],
                                   in1=coskb[:, g * 256:(g + 1) * 256], op=ALU.mult))
            krope = kb.bump("vS", V.tensor_tensor(out=kRt[:, g * 256:(g + 1) * 256],
                                                  in0=krt_s[:], in1=krh[:], op=ALU.add))
            kb.wait(V, krope)
            kb.wait(SC, krope)
            s64 = kb.bump("sS", SC.activation(out=sq64[:],
                                              in_=kRt[0:64, g * 256:(g + 1) * 256],
                                              func=AF.Square))
            kb.use_bank(3)
            kb.wait(T, s64)
            tmk = kb.bump("tS", T.matmul(pb[3][0:1, 0:256], ones_c[0:64, 0:1],
                                         sq64[:], start=True, stop=True))
            kb.wait(V, tmk)
            kb.wait(V, rowKR_mk)
            rowKR_mk = kb.bump("vS", V.tensor_copy(out=row(1, 256), in_=pb[3][0:1, 0:256]))
            kb.evac_mark(3, rowKR_mk)
            for h in range(NH):
                bk = (0, 1)[h % 2]
                kb.use_bank(bk)
                kb.wait(T, wres_mk)
                kb.wait(T, ck_mks[-1])
                prs = [(wukv_k[:, l * 2048 + h * 128:l * 2048 + (h + 1) * 128],
                        ckv[:, l * 256:(l + 1) * 256]) for l in range(4)]
                tmk = kb.bump("tS", mm_group(pb[bk][:, 0:256], prs))
                kst = kst2[h % 2]
                kb.wait(SC, tmk)
                kb.wait(SC, prev_kst[h % 2])
                kev = kb.bump("sS", SC.activation(
                    out=kst[:], in_=pb[bk][:, 0:256], func=AF.Identity,
                    bias=biases[:, 72 + h:73 + h]))
                kb.evac_mark(bk, kev)
                kb.wait(SY, kev)
                kship.append(kb.dma(SY, "qS", k_park_d[h][:, g * 256:(g + 1) * 256],
                                    kst[:]))
                prev_kst[h % 2] = kship[-1]
                kb.wait(SC, kev)
                s2 = kb.bump("sS", SC.activation(out=sq_sc[:, 0:256], in_=kst[:],
                                                 func=AF.Square))
                kb.use_bank(3)
                kb.wait(T, s2)
                tmk = kb.bump("tS", T.matmul(pb[3][0:1, 0:256], ones_c[:],
                                             sq_sc[:, 0:256], start=True, stop=True))
                kb.wait(V, tmk)
                kb.wait(V, rowKR_mk)
                vchain(V.tensor_tensor(out=row(0, 256), in0=pb[3][0:1, 0:256],
                                       in1=row(1, 256), op=ALU.add))
                kb.evac_mark(3, kb.mark("vS"))
                vchain(V.tensor_reduce(out=rows[0:1, 512:513], in_=row(0, 256),
                                       axis=AX.X, op=ALU.max))
                if g == 0:
                    km = vchain(V.tensor_copy(out=rows[0:1, 1792 + h:1793 + h],
                                              in_=rows[0:1, 512:513]))
                else:
                    km = vchain(V.tensor_tensor(
                        out=rows[0:1, 1792 + h:1793 + h],
                        in0=rows[0:1, 1792 + h:1793 + h],
                        in1=rows[0:1, 512:513], op=ALU.max))
                kmax_mks[h] = km
            for ts_ in range(2):
                vst = vst2[ts_]
                for dvb in range(4):
                    bk = (4, 5)[dvb % 2]
                    kb.use_bank(bk)
                    kb.wait(T, ck_mks[-1])
                    kb.wait(T, wres_mk)
                    kb.wait(T, vb_l)
                    kb.wait(T, ones_mk)
                    prs = [(ckv[:, l * 256 + ts_ * 128:l * 256 + ts_ * 128 + 128],
                            wukv_v[:, l * 2048 + dvb * 512:l * 2048 + dvb * 512 + 512])
                           for l in range(4)]
                    tmk = kb.bump("tS", mm_group(
                        pb[bk][:, 0:512], prs,
                        bias_k1=(ones_rb[:], vbias[0:1, dvb * 512:dvb * 512 + 512])))
                    kb.wait(SC, tmk)
                    kb.wait(SC, prev_v[ts_])
                    vev = kb.bump("sS", SC.activation(
                        out=vst[:, dvb * 512:dvb * 512 + 512], in_=pb[bk][:, 0:512],
                        func=AF.Copy))
                    kb.evac_mark(bk, vev)
                kb.wait(SY, vev)
                vship.append(kb.dma(
                    SY, "qS",
                    v_park_d[g * 256 + ts_ * 128:g * 256 + ts_ * 128 + 128, :],
                    vst[:]))
                prev_v[ts_] = vship[-1]
            xg_busy[buf] = kb.mark("tS")
        kr_done = kb.mark("vS")
        st3kv.close()

        st3o = ExitStack()
        Ob = sb("Ob", [128, NH * 1024], BF16, st3o)
        st3a = ExitStack()
        khead = [sb(f"khead{i}", [128, S], BF16, st3a) for i in range(2)]
        vhead = [sb(f"vhead{i}", [128, 16 * 128], BF16, st3a) for i in range(2)]
        qh = [sb(f"qh{i}", [128, TPC], BF16, st3a) for i in range(2)]
        qrh = [sb(f"qrh{i}", [64, TPC], BF16, st3a) for i in range(2)]
        T1 = [sb(f"T1_{i}", [128, 1024], F32, st3a) for i in range(2)]
        Pt = [sb(f"Pt_{i}", [128, 1024], BF16, st3a) for i in range(2)]
        qsq = sb("qsq", [64, TPC], BF16, st3a)
        hd_busy = [None, None]
        qn2_tmk = [None, None]
        o_mks = []
        lastk = (7, 15)
        last_exp = {0: None, 1: None}
        last_avden = {0: None, 1: None}
        for h in range(NH):
            buf = h % 2
            kb.wait(SY, hd_busy[buf])
            if h == 0:
                wait_all(SY, kship)
                wait_all(SY, vship)
                wait_all(SY, qship)
                wait_all(SY, qrship)
                kb.wait(SY, kb.mark("tS"))
            klm = kb.dma(SY, "qS", khead[buf][:], k_park_d[h])
            vlm = kb.dma(SY, "qS", vhead[buf][:],
                         bass.AP(v_park_d.tensor, v_park_d.offset + h * 128,
                                 [[D, 128], [128 * D, 16], [1, 128]]))
            qlm = kb.dma(SY, "qS", qh[buf][:], q_park_d[h])
            qrlm = kb.dma(SY, "qS", qrh[buf][:], qr_park_d[h])
            kb.wait(SC, qlm)
            kb.wait(SC, qn2_tmk[1 - buf] if h > 0 else None)
            sq1 = kb.bump("sS", SC.activation(out=sq_sc[:], in_=qh[buf][:], func=AF.Square))
            kb.wait(SC, qrlm)
            sq2 = kb.bump("sS", SC.activation(out=qsq[:], in_=qrh[buf][:], func=AF.Square))
            for half in range(2):
                bk = (6, 7)[half]
                kb.use_bank(bk)
                kb.wait(T, sq1)
                kb.wait(T, sq2)
                T.matmul(pb[bk][0:1, 0:512], ones_c[:],
                         sq_sc[:, half * 512:half * 512 + 512], start=True, stop=False)
                tmk = kb.bump("tS", T.matmul(
                    pb[bk][0:1, 0:512], ones_c[0:64, 0:1],
                    qsq[:, half * 512:half * 512 + 512], start=False, stop=True))
                kb.wait(V, tmk)
                emk = kb.bump("vS", V.tensor_copy(out=row(0, 512, half * 512),
                                                  in_=pb[bk][0:1, 0:512]))
                kb.evac_mark(bk, emk)
            qn2_tmk[buf] = kb.mark("tS")
            qn_mk = kb.mark("vS")
            kb.wait(V, qn_mk)
            kb.wait(V, kmax_mks[h])
            brow = vchain(V.tensor_scalar(
                out=row(0), in0=row(0), scalar1=rows[0:1, 1792 + h:1793 + h],
                scalar2=0.5 * RSQ_DK, op0=ALU.add, op1=ALU.mult))
            b_bc = bcast_row(row(0), bca, brow)
            av_mks = [None, None]
            den_mks = [None, None]
            exp_mks = [None] * 16

            def issue_avden(k, buf=buf):
                kp = k % 2
                W = UC[k]
                c0 = 1024 - W * 128
                kb.wait(T, exp_mks[k])
                for half, bk in ((0, 6), (1, 7)):
                    s0, s1 = half * 512, half * 512 + 512
                    if s1 <= c0:
                        continue
                    a0 = max(s0, c0)
                    if k == 0:
                        kb.use_bank(bk)
                    inst = T.matmul(pb[bk][0:1, a0 - s0:512], ones_c[:],
                                    Pt[kp][:, a0:s1], start=(k == 0),
                                    stop=(k == lastk[half]))
                    if k == lastk[half]:
                        den_mks[half] = kb.bump("tS", inst)
                for half, bk in ((0, 4), (1, 5)):
                    s0, s1 = half * 512, half * 512 + 512
                    if s1 <= c0:
                        continue
                    a0 = max(s0, c0)
                    if k == 0:
                        kb.use_bank(bk)
                    inst = T.matmul(pb[bk][:, a0 - s0:512],
                                    vhead[buf][:, k * 128:(k + 1) * 128],
                                    Pt[kp][:, a0:s1], start=(k == 0),
                                    stop=(k == lastk[half]))
                    if k == lastk[half]:
                        av_mks[half] = kb.bump("tS", inst)
                last_avden[kp] = kb.mark("tS")

            for k in range(16):
                kp = k % 2
                W = UC[k]
                c0 = 1024 - W * 128
                b0, b1 = ((0, 1), (2, 3))[kp]
                segs = []
                if c0 < 512:
                    segs.append((b0, c0, 512))
                segs.append((b1, max(c0, 512), 1024))
                kb.wait(T, klm)
                kb.wait(T, vlm)
                kb.wait(T, qlm)
                kb.wait(T, qrlm)
                if k == 0:
                    kb.wait(T, kr_done)
                sc_mks = []
                for (bk, s0, s1) in segs:
                    kb.use_bank(bk)
                    T.matmul(pb[bk][:, 0:s1 - s0], khead[buf][:, k * 128:(k + 1) * 128],
                             qh[buf][:, s0:s1], start=True, stop=False)
                    tmk = kb.bump("tS", T.matmul(
                        pb[bk][:, 0:s1 - s0], kRt[:, k * 128:(k + 1) * 128],
                        qrh[buf][:, s0:s1], start=False, stop=True))
                    sc_mks.append((bk, s0, s1, tmk))
                kb.wait(V, b_bc)
                kb.wait(V, last_exp[kp])
                t1_mk = None
                for (bk, s0, s1, tmk) in sc_mks:
                    kb.wait(V, tmk)
                    t1_mk = kb.bump("vS", V.scalar_tensor_tensor(
                        out=T1[kp][:, s0:s1], in0=pb[bk][:, 0:s1 - s0], scalar=RSQ_DK,
                        in1=bca[:, s0:s1], op0=ALU.mult, op1=ALU.subtract))
                    kb.evac_mark(bk, t1_mk)
                kb.wait(V, t1_mk)
                kb.wait(V, mload)
                msk = kb.bump("vS", V.tensor_tensor(
                    out=T1[kp][:, c0:c0 + 128], in0=T1[kp][:, c0:c0 + 128],
                    in1=masksb[:, k * 128:(k + 1) * 128], op=ALU.add))
                kb.wait(SC, msk)
                kb.wait(SC, last_avden[kp])
                exp_mks[k] = kb.bump("sS", SC.activation(out=Pt[kp][:, c0:1024],
                                                         in_=T1[kp][:, c0:1024],
                                                         func=AF.Exp))
                last_exp[kp] = exp_mks[k]
                if k >= 1:
                    issue_avden(k - 1)
            issue_avden(15)
            hd_busy[buf] = kb.mark("tS")
            kb.wait(V, den_mks[0])
            kb.wait(V, den_mks[1])
            vchain(V.tensor_copy(out=row(0, 512), in_=pb[6][0:1, 0:512]))
            dn2 = kb.bump("vS", V.tensor_copy(out=row(0, 512, 512), in_=pb[7][0:1, 0:512]))
            kb.evac_mark(6, dn2)
            kb.evac_mark(7, dn2)
            kb.wait(V, dn2)
            rcp = vchain(V.reciprocal(out=row(0), in_=row(0)))
            rd_bc = bcast_row(row(0), bcb, rcp)
            kb.wait(V, rd_bc)
            kb.wait(V, av_mks[0])
            kb.wait(V, av_mks[1])
            om = None
            for half, bk in ((0, 4), (1, 5)):
                om = kb.bump("vS", V.tensor_tensor(
                    out=Ob[:, h * 1024 + half * 512:h * 1024 + half * 512 + 512],
                    in0=pb[bk][:, 0:512], in1=bcb[:, half * 512:half * 512 + 512],
                    op=ALU.mult))
                kb.evac_mark(bk, om)
            o_mks.append(om)
        st3a.close()

        st3w = ExitStack()
        stream3 = sb("stream3", [128, NCH * 1024], F32, st3w)
        kb.wait(SY, kb.mark("tS"))
        rel3 = reload_stream(stream3)
        begin_wphase(kb.mark("tS"), dbl=False)

        def mov_o(i, half):
            return Ob[:, i * 1024 + half * 512:i * 1024 + half * 512 + 512]

        res3 = None
        for e in range(16):
            def evac_res3(half, bk, tmk, e=e):
                kb.wait(V, tmk)
                if e == 0 and half == 0:
                    wait_all(V, rel3)
                    kb.wait(V, o_mks[-1])
                emk = kb.bump("vS", V.scalar_tensor_tensor(
                    out=stream3[:, e * 1024 + half * 512:e * 1024 + half * 512 + 512],
                    in0=pb[bk][:, 0:512], scalar=biases[:, 112 + e:113 + e],
                    in1=stream3[:, e * 1024 + half * 512:e * 1024 + half * 512 + 512],
                    op0=ALU.add, op1=ALU.add))
                kb.evac_mark(bk, emk)
                return emk
            res3 = proj16(e, None, None, wo1_d, e * 128, D, mov_o,
                          o_mks[-1] if e == 0 else None, evac_f=evac_res3)
        end_wphase()
        kb.wait(SC, xship)
        ln3 = layernorm(stream3, res3)
        park3 = park_stream(stream3, ln3)
        st3w.close()
        st3o.close()
        st3.close()

    if lvl >= 4:
        stream4, ln4, st_f4 = ffn(aguA_out[1], aguB_out[1], adn_out[1], ln3,
                                  ccm["agA1"], ccm["agB1"], ccm["agD1"], park3)
        final_stream, final_ln = stream4, ln4
    elif lvl == 3:
        stream_o3 = sb("stream_o3", [128, NCH * 1024], F32)
        wait_all(SY, park3)
        rel_o3 = reload_stream(stream_o3)
        wait_all(SY, rel_o3)
        final_stream, final_ln = stream_o3, None

    if lvl == 1:
        stream2 = sb("stream_out", [128, NCH * 1024], F32)
        wait_all(SY, park1)
        rel = reload_stream(stream2)
        final_stream, final_ln = stream2, None
        wait_all(SY, rel)
    elif lvl == 0:
        final_stream, final_ln = stream, kb.mark("sS")

    if final_ln is not None:
        kb.wait(SY, final_ln)
    kb.wait(SY, kb.mark("vS"))
    kb.wait(SY, kb.mark("sS"))
    st_mks = [kb.dma(SY, "qS", y_d[c * 128:(c + 1) * 128, :],
                     final_stream[:, c * 1024:(c + 1) * 1024]) for c in range(NCH)]
    wait_all(SY, st_mks)

    if lvl == 0:
        st_main.close()
    if lvl in (2,) and not tapped[0]:
        st_f2.close()
    if lvl >= 4:
        st_f4.close()
    es.close()
    return nc


def kernel(**inputs):
    in_maps = make_in_maps(inputs)
    nc = build(_UPTO, _TAP)
    res = run_bass_kernel_spmd(nc, in_maps, core_ids=list(range(8)))
    out = np.zeros((B, S, D), dtype=np.float32)
    for c in range(8):
        out[c // 2, _core_pos(c), :] = np.asarray(res.results[c]["y"]).T
    return out


# version 11 (final)
# speedup vs baseline: 1.0225x; 1.0225x over previous
import numpy as np
import ml_dtypes
from contextlib import ExitStack

import concourse.bass as bass
import concourse.mybir as mybir
from concourse.bass_utils import run_bass_kernel_spmd

F32 = mybir.dt.float32
BF16 = mybir.dt.bfloat16
F16 = mybir.dt.float16
I32 = mybir.dt.int32
U16 = mybir.dt.uint16
AF = mybir.ActivationFunctionType
ALU = mybir.AluOpType
AX = mybir.AxisListType
BF = ml_dtypes.bfloat16

B, S, D = 4, 2048, 2048
NH, DKH, DHR, LAT = 16, 128, 64, 512
IFF = 5632
NT = 8
TPC = NT * 128
NCH = D // 128
GU_SH = 1408
DN_SH = 256
NEG = -30000.0
RSQ_DK = float(1.0 / np.sqrt(128.0))

UC = [max(len([t for t in range(p, 16, 2) if t >= k]) for p in (0, 1)) for k in range(16)]
PAIRS = [[0, 1], [2, 3], [4, 5], [6, 7]]
ALL8 = [list(range(8))]

_UPTO = "all"
_TAP = None
_ORD = ["x", "l0attn", "l0ffn", "l1attn", "all"]



def _rope_l0_tables(pos):
    rot = 1024
    inv = 1.0 / (10000.0 ** (np.arange(0, rot, 2, dtype=np.float64) / rot))
    fr = np.repeat(pos[:, None].astype(np.float64) * inv[None, :], 2, axis=-1)
    return (np.ascontiguousarray(np.cos(fr).T.astype(BF)),
            np.ascontiguousarray(np.sin(fr).T.astype(BF)))


def _rope_l1_tables(pos, d):
    pe = np.repeat(np.arange(d // 2, dtype=np.float64), 2)
    th = np.exp(-2.0 * pe / d * np.log(10000.0))
    ang = (pos[:, None].astype(np.float64) + 1.0) * th[None, :]
    return (np.ascontiguousarray(np.cos(ang).T.astype(BF)),
            np.ascontiguousarray(np.sin(ang).T.astype(BF)))


def _rot_sign_mat():
    R = np.zeros((128, 128), dtype=np.float32)
    for i in range(64):
        R[2 * i + 1, 2 * i] = -1.0
        R[2 * i, 2 * i + 1] = 1.0
    return np.ascontiguousarray(R.astype(BF))


def _bias_cols(vec, n128):
    return np.ascontiguousarray(np.asarray(vec, dtype=np.float32).reshape(n128, 128).T)


def _mk_masks(p):
    m = np.zeros((16, 128, 128), dtype=np.float32)
    tri = np.where(np.arange(128)[:, None] > np.arange(128)[None, :], NEG, 0.0)
    for k in range(16):
        own = len([t for t in range(p, 16, 2) if t >= k])
        if k % 2 == p:
            m[k] = tri
        elif UC[k] > own:
            m[k] = NEG
    return np.ascontiguousarray(m)


def _core_pos(c):
    p = c % 2
    return np.concatenate([np.arange(t * 128, t * 128 + 128) for t in range(p, 16, 2)])


def make_in_maps(inputs):
    X = np.asarray(inputs["L0_attn_X"], dtype=np.float32)
    W_inT = np.ascontiguousarray(np.asarray(inputs["L0_attn_W_in"], dtype=np.float32).T)
    W_outT = np.ascontiguousarray(np.asarray(inputs["L0_attn_W_out"], dtype=np.float32).T)
    gup = {0: np.asarray(inputs["L0_ffn_gate_up_packed"]), 1: np.asarray(inputs["L1_ffn_gate_up_packed"])}
    gus = {0: np.asarray(inputs["L0_ffn_gate_up_scales"]), 1: np.asarray(inputs["L1_ffn_gate_up_scales"])}
    dnp = {0: np.asarray(inputs["L0_ffn_down_packed"]), 1: np.asarray(inputs["L1_ffn_down_packed"])}
    dns = {0: np.asarray(inputs["L0_ffn_down_scales"]), 1: np.asarray(inputs["L1_ffn_down_scales"])}

    biases = np.concatenate([
        _bias_cols(inputs["L0_attn_b_in"], 48),
        _bias_cols(inputs["L0_attn_b_out"], 16),
        _bias_cols(inputs["L1_attn_W_down_b"], 8),
        _bias_cols(np.asarray(inputs["L1_attn_W_uKV_b"])[:2048], 16),
        _bias_cols(inputs["L1_attn_W_uQ_b"], 16),
        _bias_cols(inputs["L1_attn_Wqr_b"], 8),
        _bias_cols(inputs["L1_attn_Wout_b"], 16),
        np.pad(np.asarray(inputs["L1_attn_Wkr_b"], dtype=np.float32), (0, 64))[:, None],
        np.full((128, 1), 1e-5, dtype=np.float32),
        np.full((128, 1), 1e-6, dtype=np.float32),
    ], axis=1).astype(np.float32)

    vbias = np.ascontiguousarray(
        np.asarray(inputs["L1_attn_W_uKV_b"])[2048:].astype(BF)[None, :])
    ident = np.eye(128, dtype=np.float32)
    Rm = _rot_sign_mat()
    coskF, sinkF = _rope_l1_tables(np.arange(S), 64)

    in_maps = []
    for c in range(8):
        b, p = c // 2, c % 2
        pos = _core_pos(c)
        cos0, sin0 = _rope_l0_tables(pos)
        cosq, sinq = _rope_l1_tables(pos, 1024)
        im = {
            "x": np.ascontiguousarray(X[b, pos, :]),
            "w_inT": W_inT, "w_outT": W_outT,
            "biases": np.ascontiguousarray(biases),
            "vbias": vbias, "ident": ident, "identb": ident.astype(BF), "rmat": Rm,
            "cos0": cos0, "sin0": sin0, "cosq": cosq, "sinq": sinq,
            "cosk": coskF, "sink": sinkF,
            "masks": _mk_masks(p),
            "slot": np.array([[p]], dtype=np.int32),
            "wdn": np.ascontiguousarray(np.asarray(inputs["L1_attn_W_down_k"], dtype=np.float32)),
            "wukv": np.ascontiguousarray(np.asarray(inputs["L1_attn_W_uKV_k"], dtype=np.float32)),
            "wuq": np.ascontiguousarray(np.asarray(inputs["L1_attn_W_uQ_k"], dtype=np.float32)),
            "wkr": np.ascontiguousarray(np.asarray(inputs["L1_attn_Wkr_k"], dtype=np.float32)),
            "wqr": np.ascontiguousarray(np.asarray(inputs["L1_attn_Wqr_k"], dtype=np.float32)),
            "wo1": np.ascontiguousarray(np.asarray(inputs["L1_attn_Wout_k"], dtype=np.float32)),
        }
        for L in (0, 1):
            im[f"gup{L}"] = np.ascontiguousarray(gup[L][GU_SH * c:GU_SH * (c + 1)])
            im[f"gus{L}"] = np.ascontiguousarray(
                gus[L].T[GU_SH * c:GU_SH * (c + 1)].astype(np.float16))
            im[f"dnp{L}"] = np.ascontiguousarray(dnp[L][DN_SH * c:DN_SH * (c + 1)])
            im[f"dns{L}"] = np.ascontiguousarray(
                dns[L].T[DN_SH * c:DN_SH * (c + 1)].astype(np.float16))
        in_maps.append(im)
    return in_maps



class KB:

    DMA_POOL = {"SP": 10, "Pool": 4, "Activation": 2}

    def __init__(self, nc, es):
        self.nc = nc
        self.c = {}
        self.sems = {}
        for n in ("tS", "vS", "sS", "cc", "gS"):
            self.sems[n] = es.enter_context(nc.semaphore(n))
            self.c[n] = 0
        self.pool = {}
        self.rr = {}
        for eng, sz in self.DMA_POOL.items():
            self.pool[eng] = []
            self.rr[eng] = 0
            for i in range(sz):
                n = f"qd_{eng}_{i}"
                self.sems[n] = es.enter_context(nc.semaphore(n))
                self.c[n] = 0
                self.pool[eng].append(n)
        self.bank_free = [None] * 8

    def bump(self, name, inst, n=1):
        inst.then_inc(self.sems[name], n)
        self.c[name] += n
        return (name, self.c[name])

    def mark(self, name):
        return (name, self.c[name])

    def wait(self, eng, mk):
        if mk is None:
            return
        name, v = mk
        if v > 0:
            eng.wait_ge(self.sems[name], v)

    def dma(self, eng, qname, out, in_):
        ename = eng.engine.name if hasattr(eng, "engine") else "SP"
        pool = self.pool[ename]
        slot = pool[self.rr[ename] % len(pool)]
        self.rr[ename] += 1
        if self.c[slot] > 0:
            eng.wait_ge(self.sems[slot], self.c[slot])
        return self.bump(slot, eng.dma_start(out=out, in_=in_), 16)

    def use_bank(self, i):
        if self.bank_free[i] is not None:
            self.wait(self.nc.tensor, self.bank_free[i])
            self.bank_free[i] = None

    def evac_mark(self, i, mk):
        self.bank_free[i] = mk


def build(upto="all", tap=None):
    lvl = _ORD.index(upto)
    nc = bass.Bass("TRN2", target_bir_lowering=False)
    T, V, SC, SY, G = nc.tensor, nc.vector, nc.scalar, nc.sync, nc.gpsimd

    def din(name, shape, dt):
        return nc.dram_tensor(name, shape, dt, kind="ExternalInput").ap()

    x_d = din("x", [TPC, D], F32)
    w_inT_d = din("w_inT", [D, 3 * D], F32)
    w_outT_d = din("w_outT", [D, D], F32)
    biases_d = din("biases", [128, 131], F32)
    vbias_d = din("vbias", [1, D], BF16)
    ident_d = din("ident", [128, 128], F32)
    identb_d = din("identb", [128, 128], BF16)
    rmat_d = din("rmat", [128, 128], BF16)
    cos0_d, sin0_d = din("cos0", [1024, TPC], BF16), din("sin0", [1024, TPC], BF16)
    cosq_d, sinq_d = din("cosq", [1024, TPC], BF16), din("sinq", [1024, TPC], BF16)
    cosk_d, sink_d = din("cosk", [64, S], BF16), din("sink", [64, S], BF16)
    masks_d = din("masks", [16, 128, 128], F32)
    slot_d = din("slot", [1, 1], I32)
    wdn_d = din("wdn", [D, 1024], F32)
    wukv_d = din("wukv", [LAT, 4096], F32)
    wuq_d = din("wuq", [LAT, D], F32)
    wkr_d = din("wkr", [D, 64], F32)
    wqr_d = din("wqr", [D, 1024], F32)
    wo1_d = din("wo1", [D, D], F32)
    gup_d = {L: din(f"gup{L}", [GU_SH, 256], I32) for L in (0, 1)}
    gus_d = {L: din(f"gus{L}", [GU_SH, 16], F16) for L in (0, 1)}
    dnp_d = {L: din(f"dnp{L}", [DN_SH, 704], I32) for L in (0, 1)}
    dns_d = {L: din(f"dns{L}", [DN_SH, 44], F16) for L in (0, 1)}
    y_d = nc.dram_tensor("y", [D, TPC], F32, kind="ExternalOutput").ap()

    sh_A = nc.dram_tensor("sh_A", [2, 128, 16], F32, addr_space="Shared").ap()
    sh_x = nc.dram_tensor("sh_x", [2, D, TPC], BF16, addr_space="Shared").ap()
    bar_in = nc.dram_tensor("bar_in", [1, 16], F32).ap()
    bar_out = nc.dram_tensor("bar_out", [1, 16], F32).ap()
    aguA_in = {L: nc.dram_tensor(f"aguA_in{L}", [D, 768], BF16).ap() for L in (0, 1)}
    aguA_out = {L: nc.dram_tensor(f"aguA_out{L}", [8, D, 768], BF16).ap() for L in (0, 1)}
    aguB_in = {L: nc.dram_tensor(f"aguB_in{L}", [D, 640], BF16).ap() for L in (0, 1)}
    aguB_out = {L: nc.dram_tensor(f"aguB_out{L}", [8, D, 640], BF16).ap() for L in (0, 1)}
    adn_in = {L: nc.dram_tensor(f"adn_in{L}", [IFF, DN_SH], BF16).ap() for L in (0, 1)}
    adn_out = {L: nc.dram_tensor(f"adn_out{L}", [8, IFF, DN_SH], BF16).ap() for L in (0, 1)}
    spark_d = nc.dram_tensor("spark", [D, TPC], F32).ap()
    k_park_d = nc.dram_tensor("k_park", [NH, 128, S], BF16).ap()
    v_park_d = nc.dram_tensor("v_park", [S, D], BF16).ap()
    q_park_d = nc.dram_tensor("q_park", [NH, 128, TPC], BF16).ap()
    qr_park_d = nc.dram_tensor("qr_park", [NH, 64, TPC], BF16).ap()

    es = ExitStack()
    kb = KB(nc, es)

    def sb(name, shape, dt, st=None):
        return (st or es).enter_context(nc.sbuf_tensor(name, shape, dt))

    xb = sb("xb", [128, NCH * 1024], BF16)
    biases = sb("biases_sb", [128, 131], F32)
    ident = sb("ident_sb", [128, 128], F32)
    identb = sb("identb_sb", [128, 128], BF16)
    rmat = sb("rmat_sb", [128, 128], BF16)
    ones_c = sb("ones_c", [128, 1], BF16)
    ones_rf = sb("ones_rf", [1, 128], F32)
    ones_rb = sb("ones_rb", [1, 128], BF16)
    slot_sb = sb("slot_sb", [1, 1], I32)
    A_sb = sb("A_sb", [128, 16], F32)
    A_sb2 = sb("A_sb2", [128, 16], F32)
    rows = sb("rows", [1, 2048], F32)
    bca = sb("bca", [128, 1024], F32)
    bcb = sb("bcb", [128, 1024], F32)
    sq_sc = sb("sq_sc", [128, 1024], BF16)
    dq_pk = sb("dq_pk", [128, 704], I32)
    dq_sc = sb("dq_sc", [128, 44], F16)
    dq_t1 = sb("dq_t1", [128, 1024], I32)
    dq_t2 = sb("dq_t2", [128, 1024], I32)
    dq_u = sb("dq_u", [128, 1024], U16)
    dq_sx = sb("dq_sx", [128, 1024], F16)
    dq_w = sb("dq_w", [128, 1024], BF16)
    dq_tr = sb("dq_tr", [128, 1024], BF16)
    pb = [es.enter_context(nc.psum_tensor(f"pb{i}", [128, 512], F32)) for i in range(8)]

    def pbb(i):
        return pb[i][:].bitcast(BF16)

    def row(i, n=1024, off=0):
        return rows[0:1, i * 1024 + off:i * 1024 + off + n]

    def rowx(n=1024):
        return bca[0:1, 0:n]

    G.memset(ones_c[:], 1.0)
    G.memset(ones_rf[:], 1.0)
    ones_mk = kb.bump("gS", G.memset(ones_rb[:], 1.0))
    ld_biases = kb.dma(SY, "qS", biases[:], biases_d[:])
    ld_ident = kb.dma(SY, "qS", ident[:], ident_d[:])
    ld_identb = kb.dma(SY, "qS", identb[:], identb_d[:])
    ld_rmat = kb.dma(SY, "qS", rmat[:], rmat_d[:])
    ld_init = kb.dma(SY, "qS", slot_sb[:], slot_d[:])
    ld0 = ld_biases

    kb.wait(G, ld_init)
    rA = G.alloc_register("rAoff")
    rX = G.alloc_register("rXoff")
    G.reg_load(rA, slot_sb[0:1, 0:1])
    G.reg_load(rX, slot_sb[0:1, 0:1])
    G.reg_mul(rA, rA, 128 * 16)
    G.reg_mul(rX, rX, D * TPC)

    def vchain(inst):
        mk = kb.bump("vS", inst)
        kb.wait(V, mk)
        return mk

    def schain(inst):
        mk = kb.bump("sS", inst)
        kb.wait(SC, mk)
        return mk

    def mm_group(bank_ap, pairs, bias_k1=None):
        allp = ([bias_k1] if bias_k1 else []) + list(pairs)
        inst = None
        for i, (lh, rh) in enumerate(allp):
            inst = T.matmul(bank_ap, lh, rh, start=(i == 0), stop=(i == len(allp) - 1))
        return inst

    def bcast_row(src_row_f32, dst_sb, wait_mk, banks=(6, 7), n=1024):
        for half in range(2 if n > 512 else 1):
            w = min(512, n - half * 512)
            bk = banks[half]
            kb.use_bank(bk)
            kb.wait(T, wait_mk)
            tmk = kb.bump("tS", T.matmul(
                pb[bk][:, 0:w], ones_rf[:], src_row_f32[0:1, half * 512:half * 512 + w],
                start=True, stop=True))
            kb.wait(V, tmk)
            emk = kb.bump("vS", V.tensor_copy(out=dst_sb[:, half * 512:half * 512 + w],
                                              in_=pb[bk][:, 0:w]))
            kb.evac_mark(bk, emk)
        return kb.mark("vS")

    def sq_reduce(chunk_src, n_ch, dst_row, banks=(6, 7)):
        lred = None
        for c in range(n_ch):
            ch, wm = chunk_src(c)
            kb.wait(SC, wm)
            smk = kb.bump("sS", SC.activation(out=sq_sc[:], in_=ch, func=AF.Square))
            for half in range(2):
                bk = banks[half]
                if c == 0:
                    kb.use_bank(bk)
                kb.wait(T, smk)
                inst = T.matmul(pb[bk][0:1, 0:512], ones_c[:],
                                sq_sc[:, half * 512:half * 512 + 512],
                                start=(c == 0), stop=(c == n_ch - 1))
            lred = kb.bump("tS", inst)
            kb.wait(SC, lred)
        kb.wait(V, lred)
        kb.bump("vS", V.tensor_copy(out=dst_row[0:1, 0:512], in_=pb[banks[0]][0:1, 0:512]))
        mk = kb.bump("vS", V.tensor_copy(out=dst_row[0:1, 512:1024],
                                         in_=pb[banks[1]][0:1, 0:512]))
        kb.evac_mark(banks[0], mk)
        kb.evac_mark(banks[1], mk)
        return mk

    tapped = [False]

    def tap_store(src_ap_fn, nch=NCH, width=1024):
        kb.wait(G, kb.mark("vS"))
        kb.wait(G, kb.mark("sS"))
        kb.wait(G, kb.mark("tS"))
        last = None
        for c in range(nch):
            last = kb.dma(G, "qG", y_d[c * 128:(c + 1) * 128, 0:width], src_ap_fn(c))
        kb.wait(G, last)
        tapped[0] = True

    st_main = ExitStack()
    stream = sb("stream0", [128, NCH * 1024], F32, st_main)

    st0 = ExitStack()
    x_tm = sb("x_tm", [128, 8 * D], F32, st0)
    xld = [kb.dma(SY, "qS", x_tm[:, j * D:(j + 1) * D], x_d[j * 128:(j + 1) * 128, :])
           for j in range(NT)]
    kb.wait(SC, ld_biases)
    kb.wait(T, ones_mk)
    for j in range(NT):
        kb.wait(T, xld[j])
        if j == 0:
            kb.wait(T, ld_ident)
        for i in range(NCH):
            bk = (j * NCH + i) % 2
            kb.use_bank(bk)
            tmk = kb.bump("tS", T.transpose(
                pb[bk][:, 0:128], x_tm[:, j * D + i * 128:j * D + (i + 1) * 128], ident[:]))
            kb.wait(SC, tmk)
            emk = kb.bump("sS", SC.activation(
                out=stream[:, i * 1024 + j * 128:i * 1024 + (j + 1) * 128],
                in_=pb[bk][:, 0:128], func=AF.Copy))
            kb.evac_mark(bk, emk)
    x_trans_done = kb.mark("tS")
    kb.wait(SC, kb.mark("sS"))
    for i in range(NCH):
        kb.bump("sS", SC.activation(out=xb[:, i * 1024:(i + 1) * 1024],
                                    in_=stream[:, i * 1024:(i + 1) * 1024], func=AF.Copy))
    xb_ready = kb.mark("sS")
    st0.close()

    dq_state = {}

    def dequant(L, which, chunk_range, guard=None):
        if which == "gu":
            packed_d, scales_d = gup_d[L], gus_d[L]
            nwords, ng_ch, icols, uw = 256, 16, D, 128
        else:
            packed_d, scales_d = dnp_d[L], dns_d[L]
            nwords, ng_ch, icols, uw = 704, 44, IFF, 64
        ucols = uw * 8
        nunit = nwords // uw
        ships = []
        prev_ship = dq_state.get("ship")
        prev_add = dq_state.get("add")
        prev_tr = dq_state.get("tr")
        for ch in chunk_range:
            lp = kb.dma(G, "qG", dq_pk[:, 0:nwords],
                        packed_d[ch * 128:(ch + 1) * 128, :])
            lsc = kb.dma(G, "qG", dq_sc[:, 0:ng_ch],
                         scales_d[ch * 128:(ch + 1) * 128, :])
            if guard is not None:
                kb.wait(V, guard)
                guard = None
            for u in range(nunit):
                w = dq_pk[:, u * uw:(u + 1) * uw]
                kb.wait(V, lp)
                kb.wait(V, lsc)
                kb.wait(V, prev_ship)
                kb.wait(V, prev_add)
                sh_mk = None
                for k in range(8):
                    s1 = 9 - 4 * k
                    V.tensor_scalar(out=dq_t1[:, k * uw:(k + 1) * uw], in0=w,
                                    scalar1=abs(s1), scalar2=0x0E00,
                                    op0=ALU.logical_shift_left if s1 >= 0 else ALU.logical_shift_right,
                                    op1=ALU.bitwise_and)
                    s2 = 12 - 4 * k
                    inst = V.tensor_scalar(out=dq_t2[:, k * uw:(k + 1) * uw], in0=w,
                                           scalar1=abs(s2), scalar2=0x8000,
                                           op0=ALU.logical_shift_left if s2 >= 0 else ALU.logical_shift_right,
                                           op1=ALU.bitwise_and)
                sh_mk = kb.bump("vS", inst)
                kb.wait(V, sh_mk)
                for k in range(8):
                    inst = V.tensor_tensor(out=bass.AP(dq_u, k, [[1024, 128], [8, uw]]),
                                           in0=dq_t1[:, k * uw:(k + 1) * uw],
                                           in1=dq_t2[:, k * uw:(k + 1) * uw], op=ALU.add)
                prev_add = kb.bump("vS", inst)
                ngu = ucols // 128
                scv = bass.AP(dq_sc, u * ngu, [[44, 128], [1, ngu], [0, 128]])
                sx_mk = kb.bump("vS", V.tensor_scalar(
                    out=dq_sx[:, 0:ucols].rearrange("p (g r) -> p g r", r=128),
                    in0=scv, scalar1=16384.0, scalar2=None, op0=ALU.mult))
                kb.wait(V, sx_mk)
                kb.wait(V, prev_tr)
                vm = kb.bump("vS", V.tensor_tensor(
                    out=dq_w[:, 0:ucols], in0=dq_u[:, 0:ucols].bitcast(F16),
                    in1=dq_sx[:, 0:ucols], op=ALU.mult))
                kb.wait(SC, prev_ship)
                for it in range(ucols // 128):
                    bk = 2 + it % 2
                    kb.use_bank(bk)
                    kb.wait(T, vm)
                    kb.wait(T, ld_identb)
                    tmk = kb.bump("tS", T.transpose(
                        pbb(bk)[:, 0:128], dq_w[:, it * 128:(it + 1) * 128], identb[:]))
                    kb.wait(SC, tmk)
                    emk = kb.bump("sS", SC.activation(
                        out=dq_tr[:, it * 128:(it + 1) * 128], in_=pbb(bk)[:, 0:128],
                        func=AF.Copy))
                    kb.evac_mark(bk, emk)
                prev_tr = tmk
                kb.wait(G, kb.mark("sS"))
                if which == "gu":
                    if ch < 6:
                        dd, col0, ocols = aguA_in[L], ch * 128, 768
                    else:
                        dd, col0, ocols = aguB_in[L], (ch - 6) * 128, 640
                else:
                    dd, col0, ocols = adn_in[L], ch * 128, DN_SH
                dst = bass.AP(dd.tensor, dd.offset + (u * ucols) * ocols + col0,
                              [[ocols, 128], [128 * ocols, ucols // 128], [1, 128]])
                prev_ship = kb.dma(G, "qG", dst, dq_tr[:, 0:ucols])
                ships.append(prev_ship)
        dq_state.update(ship=prev_ship, add=prev_add, tr=prev_tr)
        return ships

    if lvl >= 2:
        gu0_shipA = dequant(0, "gu", range(0, 6), guard=x_trans_done)
        gu0_shipB = dequant(0, "gu", range(6, 11))
        dn0_ship = dequant(0, "dn", range(0, 2))

    st_w = [ExitStack()]

    wctr = [0]

    def open_wstat(dbl):
        st_w[0] = ExitStack()
        wctr[0] += 1
        n = wctr[0]
        w32a = sb(f"wstat32a_{n}", [128, D], F32, st_w[0])
        w32b = sb(f"wstat32b_{n}", [128, D], F32, st_w[0]) if dbl else w32a
        return ([w32a, w32b],
                [sb(f"wstat_a{n}", [128, D], BF16, st_w[0]),
                 sb(f"wstat_b{n}", [128, D], BF16, st_w[0])])

    wstate = {}

    def load_stat(buf, src_dram, col0, ncols, row_len, nch=NCH):
        w32, wst = wstate["w32"][buf], wstate["wst"]
        ap_src = bass.AP(src_dram.tensor, src_dram.offset + col0,
                         [[row_len, 128], [128 * row_len, nch], [1, ncols]])
        kb.wait(SY, wstate["cast32"][buf])
        if not wstate["dbl"]:
            kb.wait(SY, wstate["cast32"][1 - buf])
        lm = kb.dma(SY, "qS", w32[:, 0:nch * ncols], ap_src)
        kb.wait(V, lm)
        if wstate["busy"][buf] is not None:
            kb.wait(V, wstate["busy"][buf])
            wstate["busy"][buf] = None
        mk = kb.bump("vS", V.tensor_copy(out=wst[buf][:, 0:nch * ncols],
                                         in_=w32[:, 0:nch * ncols]))
        wstate["cast32"][buf] = mk
        return mk

    def begin_wphase(guard=None, dbl=True):
        w32, wst = open_wstat(dbl)
        wstate.update(w32=w32, wst=wst, busy=[None, None], cast32=[guard, guard],
                      dbl=dbl)

    def end_wphase():
        st_w[0].close()

    def proj16(e_idx, dst_ap, bias_col, src_dram, col0, row_len, mov, mov_wait,
               banks=(4, 5), nch=NCH, evac_f=None):
        buf = e_idx % 2
        wst = wstate["wst"]
        wm = load_stat(buf, src_dram, col0, 128, row_len, nch)
        kb.wait(T, wm)
        kb.wait(T, mov_wait)
        last = None
        for half in range(2):
            bk = banks[half]
            kb.use_bank(bk)
            pairs = [(wst[buf][:, i * 128:(i + 1) * 128], mov(i, half))
                     for i in range(nch)]
            tmk = kb.bump("tS", mm_group(pb[bk][:, 0:512], pairs))
            if half == 1:
                wstate["busy"][buf] = tmk
            if evac_f is not None:
                last = evac_f(half, bk, tmk)
            else:
                kb.wait(SC, tmk)
                last = kb.bump("sS", SC.activation(
                    out=dst_ap[:, half * 512:half * 512 + 512], in_=pb[bk][:, 0:512],
                    func=AF.Identity, bias=biases[:, bias_col:bias_col + 1]))
                kb.evac_mark(bk, last)
        return last

    def layernorm(stream_t, res_done_v):
        kb.wait(SC, res_done_v)
        zcp = []
        for i in range(NCH):
            zcp.append(kb.bump("sS", SC.activation(
                out=xb[:, i * 1024:(i + 1) * 1024],
                in_=stream_t[:, i * 1024:(i + 1) * 1024], func=AF.Copy)))
        for half in range(2):
            bk = (6, 7)[half]
            kb.use_bank(bk)
            for i in range(NCH):
                if half == 0:
                    kb.wait(T, zcp[i])
                inst = T.matmul(pb[bk][0:1, 0:512], ones_c[:],
                                xb[:, i * 1024 + half * 512:i * 1024 + half * 512 + 512],
                                start=(i == 0), stop=(i == NCH - 1))
            tmk = kb.bump("tS", inst)
            kb.wait(V, tmk)
            emk = kb.bump("vS", V.tensor_copy(out=row(0, 512, half * 512),
                                              in_=pb[bk][0:1, 0:512]))
            kb.evac_mark(bk, emk)
        sum_mk = kb.mark("vS")
        ssq_mk = sq_reduce(lambda c: (xb[:, c * 1024:(c + 1) * 1024], zcp[c]), NCH, row(1))
        kb.wait(V, sum_mk)
        kb.wait(V, ssq_mk)
        vchain(V.tensor_scalar(out=row(0), in0=row(0), scalar1=1.0 / D, scalar2=None,
                               op0=ALU.mult))
        vchain(V.tensor_scalar(out=row(1), in0=row(1), scalar1=1.0 / D, scalar2=None,
                               op0=ALU.mult))
        vchain(V.tensor_tensor(out=rowx(), in0=row(0), in1=row(0), op=ALU.mult))
        var_mk = kb.bump("vS", V.tensor_tensor(out=row(1), in0=row(1), in1=rowx(),
                                               op=ALU.subtract))
        kb.wait(SC, var_mk)
        sdm = kb.bump("sS", SC.activation(out=row(1), in_=row(1), func=AF.Sqrt, bias=biases[0:1, 129:130]))
        kb.wait(V, sdm)
        rstd_mk = vchain(V.reciprocal(out=row(1), in_=row(1)))
        vchain(V.tensor_tensor(out=row(0), in0=row(0), in1=row(1), op=ALU.mult))
        brow_mk = kb.bump("vS", V.tensor_scalar(out=row(0), in0=row(0), scalar1=-1.0,
                                                scalar2=None, op0=ALU.mult))
        a_bc = bcast_row(row(1), bca, rstd_mk)
        b_bc = bcast_row(row(0), bcb, brow_mk)
        last = None
        for c in range(NCH):
            if c == 0:
                kb.wait(V, a_bc)
                kb.wait(V, b_bc)
            vchain(V.tensor_tensor(out=stream_t[:, c * 1024:(c + 1) * 1024],
                                   in0=stream_t[:, c * 1024:(c + 1) * 1024], in1=bca[:],
                                   op=ALU.mult))
            vchain(V.tensor_tensor(out=stream_t[:, c * 1024:(c + 1) * 1024],
                                   in0=stream_t[:, c * 1024:(c + 1) * 1024], in1=bcb[:],
                                   op=ALU.add))
            last = kb.bump("vS", V.tensor_copy(out=xb[:, c * 1024:(c + 1) * 1024],
                                               in_=stream_t[:, c * 1024:(c + 1) * 1024]))
        return last

    def park_stream(stream_t, ready_v):
        kb.wait(SY, ready_v)
        return [kb.dma(SY, "qS", spark_d[c * 128:(c + 1) * 128, :],
                       stream_t[:, c * 1024:(c + 1) * 1024]) for c in range(NCH)]

    def reload_stream(stream_t):
        return [kb.dma(SY, "qS", stream_t[:, c * 1024:(c + 1) * 1024],
                       spark_d[c * 128:(c + 1) * 128, :]) for c in range(NCH)]

    def wait_all(eng, mks):
        seen = {}
        for mk in mks:
            if mk is not None:
                seen[mk[0]] = max(seen.get(mk[0], 0), mk[1])
        for name, v in seen.items():
            eng.wait_ge(kb.sems[name], v)

    a_ship = None
    if lvl >= 1:
        begin_wphase(x_trans_done)
        st1 = ExitStack()
        qkv = sb("qkv", [128, 16 * 1024], BF16, st1)
        vtmp = sb("vtmp", [128, 2 * 1024], BF16, st1)
        tabs = sb("tabs", [128, 2048], BF16, st1)
        rtmp = sb("rtmp", [128, 1024], BF16, st1)
        rtmp2 = sb("rtmp2", [128, 1024], BF16, st1)
        ap_part = sb("ap_part", [128, 16], F32, st1)

        def mov_xb(i, half):
            return xb[:, i * 1024 + half * 512:i * 1024 + half * 512 + 512]

        k_mks = [proj16(e, qkv[:, (e - 16) * 1024:(e - 15) * 1024], e,
                        w_inT_d, e * 128, 3 * D, mov_xb,
                        xb_ready if e == 16 else None)
                 for e in range(16, 32)]

        def rope8(chunk_ap, cosd, sind, chunk_ready, bank=6, guard0=None):
            done = []
            tl_prev = guard0
            for cch in range(8):
                kb.wait(SY, tl_prev)
                tl1 = kb.dma(SY, "qS", tabs[:, 0:1024], cosd[cch * 128:(cch + 1) * 128, :])
                tl2 = kb.dma(SY, "qS", tabs[:, 1024:2048], sind[cch * 128:(cch + 1) * 128, :])
                ch = chunk_ap(cch)
                kb.wait(T, chunk_ready[cch])
                kb.wait(T, ld_rmat)
                for half in range(2):
                    kb.use_bank(bank)
                    tmk = kb.bump("tS", T.matmul(
                        pb[bank][:, 0:512], rmat[:], ch[:, half * 512:half * 512 + 512],
                        start=True, stop=True))
                    kb.wait(SC, tmk)
                    kb.wait(SC, tl_prev)
                    emk = kb.bump("sS", SC.activation(
                        out=rtmp[:, half * 512:half * 512 + 512], in_=pb[bank][:, 0:512],
                        func=AF.Copy))
                    kb.evac_mark(bank, emk)
                kb.wait(V, kb.mark("sS"))
                kb.wait(V, tl1)
                kb.wait(V, tl2)
                V.tensor_tensor(out=rtmp[:], in0=rtmp[:], in1=tabs[:, 1024:2048], op=ALU.mult)
                r2 = kb.bump("vS", V.tensor_tensor(out=rtmp2[:], in0=ch,
                                                   in1=tabs[:, 0:1024], op=ALU.mult))
                kb.wait(V, r2)
                dm = kb.bump("vS", V.tensor_tensor(out=ch, in0=rtmp[:], in1=rtmp2[:],
                                                   op=ALU.add))
                tl_prev = dm
                done.append(dm)
            return done

        if tap == "kraw":
            tap_store(lambda c: qkv[:, c * 1024:(c + 1) * 1024])
            st1.close(); end_wphase(); st_main.close(); es.close()
            return nc
        ropeK = rope8(lambda c: qkv[:, c * 1024:(c + 1) * 1024], cos0_d, sin0_d, k_mks)
        if tap == "k":
            tap_store(lambda c: qkv[:, c * 1024:(c + 1) * 1024])
            st1.close(); end_wphase(); st_main.close(); es.close()
            return nc

        def ksrc(c):
            return (qkv[:, c * 1024:(c + 1) * 1024], ropeK[c] if c < 8 else k_mks[c])

        nk = sq_reduce(ksrc, 16, row(0))
        kb.wait(SC, nk)
        sqm = kb.bump("sS", SC.activation(out=row(1), in_=row(0), func=AF.Sqrt))
        kb.wait(V, sqm)
        vchain(V.tensor_scalar(out=row(1), in0=row(1), scalar1=1e-12, scalar2=None,
                               op0=ALU.max))
        rn_mk = kb.bump("vS", V.reciprocal(out=row(1), in_=row(1)))
        kb.wait(V, rn_mk)
        rn_bc = bcast_row(row(1), bca, rn_mk)
        if tap == "rnorm":
            kb.wait(SY, rn_bc)
            tap_store(lambda c: bca[:, 0:1024], nch=1)
            st1.close(); end_wphase(); st_main.close(); es.close()
            return nc

        pa_mks = []
        vconsume = [None, None]
        for c in range(16):
            def evac_v(half, bk, tmk, c=c):
                kb.wait(SC, tmk)
                emk = kb.bump("sS", SC.activation(
                    out=vtmp[:, (c % 2) * 1024 + half * 512:(c % 2) * 1024 + half * 512 + 512],
                    in_=pb[bk][:, 0:512], func=AF.Identity, bias=biases[:, 32 + c:33 + c]))
                kb.evac_mark(bk, emk)
                return emk
            vev = proj16(32 + c, None, None, w_inT_d, (32 + c) * 128, 3 * D, mov_xb,
                         vconsume[c % 2], evac_f=evac_v)
            kb.wait(V, vev)
            if c == 0:
                kb.wait(V, rn_bc)
            if pa_mks:
                kb.wait(V, pa_mks[-1])
            vchain(V.tensor_tensor(out=rtmp[:], in0=qkv[:, c * 1024:(c + 1) * 1024],
                                   in1=vtmp[:, (c % 2) * 1024:(c % 2 + 1) * 1024],
                                   op=ALU.mult))
            vchain(V.tensor_tensor(out=rtmp[:], in0=rtmp[:], in1=bca[:], op=ALU.mult))
            pa = kb.bump("vS", V.tensor_reduce(
                out=ap_part[:, c:c + 1], in_=rtmp[:], axis=AX.X, op=ALU.add))
            pa_mks.append(pa)
            vconsume[c % 2] = pa
        kb.wait(G, pa_mks[-1])
        a_ship = kb.dma(G, "qG", bass.AP(sh_A.tensor, rA, [[16, 128], [1, 16]]),
                        ap_part[:])

        q_mks = [proj16(e, qkv[:, e * 1024:(e + 1) * 1024], e,
                        w_inT_d, e * 128, 3 * D, mov_xb,
                        pa_mks[-1] if e == 0 else None)
                 for e in range(16)]
        if tap == "q3":
            kb.wait(G, q_mks[-1])
            for i in range(16):
                mm = kb.dma(G, "qG", y_d[i * 128:(i + 1) * 128, :],
                            qkv[:, i * 1024:(i + 1) * 1024])
            kb.wait(G, mm)
            st1.close(); end_wphase(); st_main.close(); es.close()
            return nc
        ropeQ = rope8(lambda c: qkv[:, c * 1024:(c + 1) * 1024], cos0_d, sin0_d, q_mks,
                      guard0=pa_mks[-1])

    ccm = {}
    if lvl >= 2:
        kb.wait(G, gu0_shipA[-1])
        ccm["agA0"] = kb.bump("cc", G.collective_compute(
            "AllGather", ALU.bypass, replica_groups=ALL8,
            ins=[aguA_in[0][:]], outs=[aguA_out[0][:]]))
    if lvl >= 1:
        kb.wait(G, a_ship)
        bi = kb.dma(G, "qG", bar_in[:], biases[0:1, 0:16])
        kb.wait(G, bi)
        ccm["bar1"] = kb.bump("cc", G.collective_compute(
            "AllReduce", ALU.add, replica_groups=PAIRS,
            ins=[bar_in[:]], outs=[bar_out[:]]))
    if lvl >= 2:
        kb.wait(G, gu0_shipB[-1])
        ccm["agB0"] = kb.bump("cc", G.collective_compute(
            "AllGather", ALU.bypass, replica_groups=ALL8,
            ins=[aguB_in[0][:]], outs=[aguB_out[0][:]]))
        kb.wait(G, dn0_ship[-1])
        ccm["agD0"] = kb.bump("cc", G.collective_compute(
            "AllGather", ALU.bypass, replica_groups=ALL8,
            ins=[adn_in[0][:]], outs=[adn_out[0][:]]))

    if tap == "wgu" and lvl >= 2:
        kb.wait(G, ("cc", kb.c["cc"]))
        for i in range(16):
            m1 = kb.dma(G, "qG", dq_w[:, 0:768],
                        aguA_out[0][0][i * 128:(i + 1) * 128, :])
            kb.wait(G, m1)
            m2 = kb.dma(G, "qG", y_d[i * 128:(i + 1) * 128, 0:768], dq_w[:, 0:768])
            kb.wait(G, m2)
        for stx in (st1, ):
            stx.close()
        end_wphase()
        st_main.close()
        es.close()
        return nc

    if lvl >= 1:
        kb.wait(SY, ccm["bar1"])
        a0 = kb.dma(SY, "qS", A_sb[:], sh_A[0])
        a1 = kb.dma(SY, "qS", A_sb2[:], sh_A[1])
        kb.wait(V, a0)
        kb.wait(V, a1)
        A_mk = kb.bump("vS", V.tensor_tensor(out=A_sb[:], in0=A_sb[:], in1=A_sb2[:],
                                             op=ALU.add))
        if tap == "A":
            kb.wait(SY, A_mk)
            tap_store(lambda c: A_sb[:, 0:16], nch=1, width=16)
            st1.close(); end_wphase(); st_main.close(); es.close()
            return nc
        y_mks = []
        for c in range(16):
            if c == 0:
                kb.wait(V, A_mk)
            kb.wait(V, ropeQ[c] if c < 8 else q_mks[c])
            y_mks.append(kb.bump("vS", V.tensor_scalar(
                out=qkv[:, c * 1024:(c + 1) * 1024], in0=qkv[:, c * 1024:(c + 1) * 1024],
                scalar1=A_sb[:, c:c + 1], scalar2=None, op0=ALU.mult)))

        if tap == "q2":
            kb.wait(G, y_mks[-1])
            for i in range(16):
                mm = kb.dma(G, "qG", y_d[i * 128:(i + 1) * 128, :],
                            qkv[:, i * 1024:(i + 1) * 1024])
            kb.wait(G, mm)
            st1.close(); end_wphase(); st_main.close(); es.close()
            return nc

        def mov_yy(i, half):
            return qkv[:, i * 1024 + half * 512:i * 1024 + half * 512 + 512]

        dbl_mk = None
        for c in range(NCH):
            dbl_mk = kb.bump("vS", V.tensor_tensor(
                out=stream[:, c * 1024:(c + 1) * 1024],
                in0=stream[:, c * 1024:(c + 1) * 1024],
                in1=stream[:, c * 1024:(c + 1) * 1024], op=ALU.add))
        kb.wait(V, dbl_mk)

        res_last = None
        for e in range(16):
            def evac_res(half, bk, tmk, e=e):
                kb.wait(V, tmk)
                emk = kb.bump("vS", V.scalar_tensor_tensor(
                    out=stream[:, e * 1024 + half * 512:e * 1024 + half * 512 + 512],
                    in0=pb[bk][:, 0:512], scalar=biases[:, 48 + e:49 + e],
                    in1=stream[:, e * 1024 + half * 512:e * 1024 + half * 512 + 512],
                    op0=ALU.add, op1=ALU.add))
                kb.evac_mark(bk, emk)
                return emk
            res_last = proj16(e, None, None, w_outT_d, e * 128, D, mov_yy,
                              y_mks[-1] if e == 0 else None, evac_f=evac_res)
        if tap == "q":
            kb.wait(SY, y_mks[-1])
            tap_store(lambda c: qkv[:, c * 1024:(c + 1) * 1024])
            st1.close()
            end_wphase()
            st_main.close()
            es.close()
            return nc
        st1.close()
        end_wphase()
        if tap == "yattn":
            kb.wait(SY, res_last)
            tap_store(lambda c: stream[:, c * 1024:(c + 1) * 1024])
            st_main.close()
            es.close()
            return nc
        ln1 = layernorm(stream, res_last)
        park1 = park_stream(stream, ln1)
        st_main.close()

    def ffn(agA_t, agB_t, adn_t, in_mk, agA_mk, agB_mk, agD_mk, park_mk):
        fid = ffn.ctr
        stF = ExitStack()
        h = sb(f"h_ffn{fid}", [128, 44 * 1024], BF16, stF)
        stGU = ExitStack()
        wg = sb(f"wg_ffn{fid}", [128, NCH * 256], BF16, stGU)
        wu = sb(f"wu_ffn{fid}", [128, NCH * 256], BF16, stGU)
        sg = sb(f"sg_ffn{fid}", [128, 1024], BF16, stGU)
        ub = sb(f"ub_ffn{fid}", [128, 1024], BF16, stGU)
        wg_busy = [None]
        wu_busy = [None]
        first_guard = [park_mk]

        def stat_load2(dst, out_t, rank, j0, nj, busy):
            ocols = int(out_t.tensor.shape[2])
            kb.wait(SY, busy[0])
            if first_guard[0] is not None:
                wait_all(SY, first_guard[0])
                first_guard[0] = None
            src = bass.AP(out_t.tensor, out_t.offset + rank * D * ocols + j0 * 128,
                          [[ocols, 128], [128 * ocols, NCH], [1, nj * 128]])
            return kb.dma(SY, "qS", dst[:, 0:NCH * nj * 128], src)

        hm = None
        for part, agt, agmk in (("A", agA_t, agA_mk), ("B", agB_t, agB_mk)):
            jlist = [(0, 2), (2, 2), (4, 2)] if part == "A" else [(0, 2), (2, 2), (4, 1)]
            for r in range(4):
                for (j0, nj) in jlist:
                    if r == 0 and j0 == 0:
                        kb.wait(SY, agmk)
                    gm = stat_load2(wg, agt, r, j0, nj, wg_busy)
                    um = stat_load2(wu, agt, r + 4, j0, nj, wu_busy)
                    for sub in range(nj):
                        hidx = 11 * r + (j0 + sub if part == "A" else 6 + j0 + sub)
                        mk_pairs = {}
                        for which, wsb, wm, bks in (("g", wg, gm, (0, 1)),
                                                    ("u", wu, um, (4, 5))):
                            tmks = []
                            for half in range(2):
                                bk = bks[half]
                                kb.use_bank(bk)
                                kb.wait(T, wm)
                                if in_mk is not None:
                                    kb.wait(T, in_mk)
                                prs = [(wsb[:, i * nj * 128 + sub * 128:i * nj * 128 + sub * 128 + 128],
                                        xb[:, i * 1024 + half * 512:i * 1024 + half * 512 + 512])
                                       for i in range(NCH)]
                                tmks.append(kb.bump("tS", mm_group(pb[bk][:, 0:512], prs)))
                            mk_pairs[which] = tmks
                        in_mk = None
                        wg_busy[0] = mk_pairs["g"][-1]
                        wu_busy[0] = mk_pairs["u"][-1]
                        kb.wait(SC, hm)
                        for half in range(2):
                            kb.wait(SC, mk_pairs["g"][half])
                            m1 = kb.bump("sS", SC.activation(
                                out=sg[:, half * 512:half * 512 + 512],
                                in_=pb[(0, 1)[half]][:, 0:512], func=AF.Silu))
                            kb.evac_mark((0, 1)[half], m1)
                            kb.wait(SC, mk_pairs["u"][half])
                            m2 = kb.bump("sS", SC.activation(
                                out=ub[:, half * 512:half * 512 + 512],
                                in_=pb[(4, 5)[half]][:, 0:512], func=AF.Copy))
                            kb.evac_mark((4, 5)[half], m2)
                        kb.wait(V, kb.mark("sS"))
                        hm = kb.bump("vS", V.tensor_tensor(
                            out=h[:, hidx * 1024:(hidx + 1) * 1024], in0=sg[:], in1=ub[:],
                            op=ALU.mult))
        if tap and tap.startswith("h"):
            base = int(tap[1:] or 0)
            kb.wait(G, hm)
            for i in range(16):
                hc = min(base + i, 43)
                mm = kb.dma(G, "qG", y_d[i * 128:(i + 1) * 128, :],
                            h[:, hc * 1024:(hc + 1) * 1024])
            kb.wait(G, mm)
            tapped[0] = True
            stGU.close()
            stF.close()
            return None, None, None
        stGU.close()
        stD = ExitStack()
        wd = sb(f"wd_ffn{fid}", [128, 44 * 256], BF16, stD)
        wd_busy = [None]
        wd_guard = [kb.mark("vS")]
        y_done = None
        for e2 in range(8):
            if e2 == 0:
                kb.wait(SY, agD_mk)
                kb.wait(SY, wd_guard[0])
            kb.wait(SY, wd_busy[0])
            src = bass.AP(adn_t.tensor, adn_t.offset + e2 * IFF * DN_SH,
                          [[DN_SH, 128], [128 * DN_SH, 44], [1, 256]])
            dml = kb.dma(SY, "qS", wd[:], src)
            for sub in range(2):
                oc = 2 * e2 + sub
                kb.wait(T, dml)
                if e2 == 0 and sub == 0:
                    kb.wait(T, hm)
                for half in range(2):
                    bk = (0, 1)[half]
                    kb.use_bank(bk)
                    prs = [(wd[:, i * 256 + sub * 128:i * 256 + sub * 128 + 128],
                            h[:, i * 1024 + half * 512:i * 1024 + half * 512 + 512])
                           for i in range(44)]
                    tmk = kb.bump("tS", mm_group(pb[bk][:, 0:512], prs))
                    if half == 1:
                        wd_busy[0] = tmk
                    kb.wait(SC, tmk)
                    emk = kb.bump("sS", SC.activation(
                        out=xb[:, oc * 1024 + half * 512:oc * 1024 + half * 512 + 512],
                        in_=pb[bk][:, 0:512], func=AF.Copy))
                    kb.evac_mark(bk, emk)
                    y_done = emk
        if tap == "yffn":
            kb.wait(G, y_done)
            kb.wait(G, ("sS", kb.c["sS"]))
            for i in range(16):
                mm = kb.dma(G, "qG", y_d[i * 128:(i + 1) * 128, :],
                            xb[:, i * 1024:(i + 1) * 1024])
            kb.wait(G, mm)
            tapped[0] = True
            stD.close()
            stF.close()
            return None, None, None
        stD.close()
        stF.close()
        st_f = ExitStack()
        stream_t = sb(f"stream_f{ffn.ctr}", [128, NCH * 1024], F32, st_f)
        ffn.ctr += 1
        ssq = sq_reduce(lambda c: (xb[:, c * 1024:(c + 1) * 1024],
                                   y_done if c == 0 else None), NCH, row(0))
        kb.wait(SY, y_done)
        rel = reload_stream(stream_t)
        kb.wait(V, ssq)
        vm = kb.bump("vS", V.tensor_scalar(out=row(0), in0=row(0), scalar1=1.0 / D,
                                           scalar2=None, op0=ALU.mult))
        kb.wait(SC, vm)
        sdm = kb.bump("sS", SC.activation(out=row(0), in_=row(0), func=AF.Sqrt, bias=biases[0:1, 130:131]))
        kb.wait(V, sdm)
        rr = vchain(V.reciprocal(out=row(0), in_=row(0)))
        r_bc = bcast_row(row(0), bca, rr)
        last = None
        for c in range(NCH):
            if c == 0:
                kb.wait(V, r_bc)
                wait_all(V, rel)
            vchain(V.tensor_tensor(out=sq_sc[:], in0=xb[:, c * 1024:(c + 1) * 1024],
                                   in1=bca[:], op=ALU.mult))
            last = kb.bump("vS", V.tensor_tensor(
                out=stream_t[:, c * 1024:(c + 1) * 1024],
                in0=stream_t[:, c * 1024:(c + 1) * 1024], in1=sq_sc[:], op=ALU.add))
            kb.wait(V, last)
        ln = layernorm(stream_t, last)
        return stream_t, ln, st_f

    ffn.ctr = 0

    if lvl >= 3:
        gu1_shipA = dequant(1, "gu", range(0, 6))
        gu1_shipB = dequant(1, "gu", range(6, 11))
        dn1_ship = dequant(1, "dn", range(0, 2))

    if lvl >= 2:
        stream2, ln2, st_f2 = ffn(aguA_out[0], aguB_out[0], adn_out[0], ln1,
                                  ccm["agA0"], ccm["agB0"], ccm["agD0"], park1)
        if tapped[0]:
            es.close()
            return nc
        final_stream, final_ln = stream2, ln2

    if lvl >= 3:
        kb.wait(G, ln2)
        xship = kb.dma(G, "qG", bass.AP(sh_x.tensor, rX,
                                        [[TPC, 128], [128 * TPC, NCH], [1, TPC]]),
                       xb[:])
        kb.wait(G, gu1_shipA[-1])
        ccm["agA1"] = kb.bump("cc", G.collective_compute(
            "AllGather", ALU.bypass, replica_groups=ALL8,
            ins=[aguA_in[1][:]], outs=[aguA_out[1][:]]))
        kb.wait(G, xship)
        bi2 = kb.dma(G, "qG", bar_in[:], biases[0:1, 0:16])
        kb.wait(G, bi2)
        ccm["bar2"] = kb.bump("cc", G.collective_compute(
            "AllReduce", ALU.add, replica_groups=PAIRS,
            ins=[bar_in[:]], outs=[bar_out[:]]))
        kb.wait(G, gu1_shipB[-1])
        ccm["agB1"] = kb.bump("cc", G.collective_compute(
            "AllGather", ALU.bypass, replica_groups=ALL8,
            ins=[aguB_in[1][:]], outs=[aguB_out[1][:]]))
        kb.wait(G, dn1_ship[-1])
        ccm["agD1"] = kb.bump("cc", G.collective_compute(
            "AllGather", ALU.bypass, replica_groups=ALL8,
            ins=[adn_in[1][:]], outs=[adn_out[1][:]]))

        park2 = park_stream(stream2, ln2)
        st_f2.close()

        st3 = ExitStack()
        masksb = sb("masksb", [128, 16 * 128], F32, st3)
        coskb = sb("coskb", [64, S], BF16, st3)
        sinkb = sb("sinkb", [64, S], BF16, st3)
        kRt = sb("kRt", [64, S], BF16, st3)
        mload = kb.dma(SY, "qS", masksb[:],
                       bass.AP(masks_d.tensor, masks_d.offset,
                               [[128, 128], [128 * 128, 16], [1, 128]]))
        ck_l = kb.dma(SY, "qS", coskb[:], cosk_d[:])
        sk_l = kb.dma(SY, "qS", sinkb[:], sink_d[:])
        wait_all(SY, park2)

        begin_wphase(None, dbl=False)
        st3q = ExitStack()
        cq = sb("cq", [128, 4 * 1024], BF16, st3q)
        qrb = sb("qrb", [128, 8 * 1024], BF16, st3q)
        tabs3 = sb("tabs3", [128, 2048], BF16, st3q)
        rtmp3 = sb("rtmp3", [128, 1024], BF16, st3q)
        rtmp4 = sb("rtmp4", [128, 1024], BF16, st3q)
        qtmp = sb("qtmp", [128, 1024], BF16, st3q)

        def mov_xb2(i, half):
            return xb[:, i * 1024 + half * 512:i * 1024 + half * 512 + 512]

        cq_mks = [proj16(l, cq[:, l * 1024:(l + 1) * 1024], 68 + l,
                         wdn_d, 512 + l * 128, 1024, mov_xb2,
                         ln2 if l == 0 else None)
                  for l in range(4)]

        def mov_cq(i, half):
            return cq[:, i * 1024 + half * 512:i * 1024 + half * 512 + 512]

        qtmp2 = [qtmp, rtmp3]
        qship = []
        for h in range(NH):
            kb.wait(SY, qship[h - 2] if h >= 2 else None)
            qm = proj16(h, qtmp2[h % 2], 88 + h, wuq_d, h * 128, D, mov_cq,
                        cq_mks[-1] if h == 0 else None, nch=4)
            kb.wait(SY, qm)
            qship.append(kb.dma(SY, "qS", q_park_d[h], qtmp2[h % 2][:]))

        qr_mks = [proj16(c, qrb[:, c * 1024:(c + 1) * 1024], 104 + c,
                         wqr_d, c * 128, 1024, mov_xb2, None)
                  for c in range(8)]

        ropeQ3 = []
        tl_prev = qship[-1]
        for cch in range(8):
            kb.wait(SY, tl_prev)
            tl1 = kb.dma(SY, "qS", tabs3[:, 0:1024], cosq_d[cch * 128:(cch + 1) * 128, :])
            tl2 = kb.dma(SY, "qS", tabs3[:, 1024:2048], sinq_d[cch * 128:(cch + 1) * 128, :])
            ch = qrb[:, cch * 1024:(cch + 1) * 1024]
            kb.wait(T, qr_mks[cch])
            kb.wait(T, ld_rmat)
            for half in range(2):
                kb.use_bank(6)
                tmk = kb.bump("tS", T.matmul(
                    pb[6][:, 0:512], rmat[:], ch[:, half * 512:half * 512 + 512],
                    start=True, stop=True))
                kb.wait(SC, tmk)
                kb.wait(SC, tl_prev)
                emk = kb.bump("sS", SC.activation(
                    out=rtmp3[:, half * 512:half * 512 + 512], in_=pb[6][:, 0:512],
                    func=AF.Copy))
                kb.evac_mark(6, emk)
            kb.wait(V, kb.mark("sS"))
            kb.wait(V, tl1)
            kb.wait(V, tl2)
            V.tensor_tensor(out=rtmp3[:], in0=rtmp3[:], in1=tabs3[:, 1024:2048], op=ALU.mult)
            r2 = kb.bump("vS", V.tensor_tensor(out=rtmp4[:], in0=ch,
                                               in1=tabs3[:, 0:1024], op=ALU.mult))
            kb.wait(V, r2)
            dm = kb.bump("vS", V.tensor_tensor(out=ch, in0=rtmp3[:], in1=rtmp4[:],
                                               op=ALU.add))
            tl_prev = dm
            ropeQ3.append(dm)
        qrship = []
        for c in range(8):
            kb.wait(SY, ropeQ3[c])
            qrship.append(kb.dma(SY, "qS", qr_park_d[2 * c],
                                 qrb[0:64, c * 1024:(c + 1) * 1024]))
            qrship.append(kb.dma(SY, "qS", qr_park_d[2 * c + 1],
                                 qrb[64:128, c * 1024:(c + 1) * 1024]))
        st3q.close()
        end_wphase()

        wait_all(SY, qship)
        wait_all(SY, qrship)
        st3kv = ExitStack()
        xg = [sb(f"xg{i}", [128, NCH * 256], BF16, st3kv) for i in range(2)]
        ckv = sb("ckv", [128, 4 * 256], BF16, st3kv)
        krt_s = sb("krt_s", [64, 256], BF16, st3kv)
        krh = sb("krh", [64, 256], BF16, st3kv)
        kst2 = [sb(f"kst{i}", [128, 256], BF16, st3kv) for i in range(2)]
        vst2 = [sb(f"vst{i}", [128, 2048], BF16, st3kv) for i in range(2)]
        wukv_v = sb("wukv_v", [128, 4 * 2048], BF16, st3kv)
        wukv_k = sb("wukv_k", [128, 4 * 2048], BF16, st3kv)
        wdn_r = sb("wdn_r", [128, 4 * 2048], BF16, st3kv)
        wkr_r = sb("wkr_r", [128, 16 * 64], BF16, st3kv)
        sq64 = sb("sq64", [64, 256], BF16, st3kv)
        vbias = sb("vbias_sb", [1, D], BF16, st3kv)
        wv32 = sb("wv32", [128, 2048], F32, st3kv)
        vb_l = kb.dma(SY, "qS", vbias[:], vbias_d[:])
        wvm = None
        res_loads = []
        for l in range(4):
            res_loads.append((wukv_v[:, l * 2048:(l + 1) * 2048],
                              bass.AP(wukv_d.tensor,
                                      wukv_d.offset + l * 128 * 4096 + 2048,
                                      [[4096, 128], [1, 2048]]), 2048))
        for l in range(4):
            res_loads.append((wukv_k[:, l * 2048:(l + 1) * 2048],
                              bass.AP(wukv_d.tensor, wukv_d.offset + l * 128 * 4096,
                                      [[4096, 128], [1, 2048]]), 2048))
        for l in range(4):
            res_loads.append((wdn_r[:, l * 2048:(l + 1) * 2048],
                              bass.AP(wdn_d.tensor, wdn_d.offset + l * 128,
                                      [[1024, 128], [128 * 1024, 16], [1, 128]]), 2048))
        res_loads.append((wkr_r[:, 0:1024],
                          bass.AP(wkr_d.tensor, wkr_d.offset,
                                  [[64, 128], [128 * 64, 16], [1, 64]]), 1024))
        for dst, src, w in res_loads:
            kb.wait(SY, wvm)
            lm = kb.dma(SY, "qS", wv32[:, 0:w], src)
            kb.wait(V, lm)
            kb.wait(V, wvm)
            wvm = kb.bump("vS", V.tensor_copy(out=dst, in_=wv32[:, 0:w]))
        wres_mk = wvm
        xbar = ccm["bar2"]
        kmax_mks = [None] * NH
        vship, kship = [], []
        xg_busy = [None, None]
        prev_v = [None, None]
        prev_kst = [None, None]
        rowKR_mk = None
        vev = None
        for g in range(8):
            buf = g % 2
            kb.wait(SY, xbar)
            kb.wait(SY, xg_busy[buf])
            xl = [kb.dma(SY, "qS", bass.AP(xg[buf], par * 128,
                                           [[NCH * 256, 128], [256, NCH], [1, 128]]),
                         bass.AP(sh_x.tensor, sh_x.offset + par * D * TPC + g * 128,
                                 [[TPC, 128], [128 * TPC, NCH], [1, 128]]))
                  for par in range(2)]
            ck_mks = []
            for l in range(4):
                kb.wait(T, wres_mk)
                kb.wait(T, xl[0])
                kb.wait(T, xl[1])
                bk = (0, 1)[l % 2]
                kb.use_bank(bk)
                prs = [(wdn_r[:, l * 2048 + i * 128:l * 2048 + (i + 1) * 128],
                        xg[buf][:, i * 256:(i + 1) * 256]) for i in range(NCH)]
                tmk = kb.bump("tS", mm_group(pb[bk][:, 0:256], prs))
                kb.wait(SC, tmk)
                kb.wait(SC, vev)
                emk = kb.bump("sS", SC.activation(
                    out=ckv[:, l * 256:(l + 1) * 256], in_=pb[bk][:, 0:256],
                    func=AF.Identity, bias=biases[:, 64 + l:65 + l]))
                kb.evac_mark(bk, emk)
                ck_mks.append(emk)
            kb.use_bank(2)
            prs = [(wkr_r[:, i * 64:(i + 1) * 64],
                    xg[buf][:, i * 256:(i + 1) * 256]) for i in range(NCH)]
            tmk = kb.bump("tS", mm_group(pb[2][0:64, 0:256], prs))
            kb.wait(SC, tmk)
            kr_mk = kb.bump("sS", SC.activation(
                out=krt_s[:], in_=pb[2][0:64, 0:256], func=AF.Identity,
                bias=biases[0:64, 128:129]))
            kb.evac_mark(2, kr_mk)
            kb.wait(T, kr_mk)
            kb.wait(T, ld_rmat)
            kb.use_bank(3)
            tmk = kb.bump("tS", T.matmul(pb[3][0:64, 0:256], rmat[0:64, 0:64],
                                         krt_s[:], start=True, stop=True))
            kb.wait(SC, tmk)
            rh_mk = kb.bump("sS", SC.activation(out=krh[:], in_=pb[3][0:64, 0:256],
                                                func=AF.Copy))
            kb.evac_mark(3, rh_mk)
            kb.wait(V, rh_mk)
            kb.wait(V, ck_l)
            kb.wait(V, sk_l)
            vchain(V.tensor_tensor(out=krh[:], in0=krh[:],
                                   in1=sinkb[:, g * 256:(g + 1) * 256], op=ALU.mult))
            vchain(V.tensor_tensor(out=krt_s[:], in0=krt_s[:],
                                   in1=coskb[:, g * 256:(g + 1) * 256], op=ALU.mult))
            krope = kb.bump("vS", V.tensor_tensor(out=kRt[:, g * 256:(g + 1) * 256],
                                                  in0=krt_s[:], in1=krh[:], op=ALU.add))
            kb.wait(V, krope)
            kb.wait(SC, krope)
            s64 = kb.bump("sS", SC.activation(out=sq64[:],
                                              in_=kRt[0:64, g * 256:(g + 1) * 256],
                                              func=AF.Square))
            kb.use_bank(3)
            kb.wait(T, s64)
            tmk = kb.bump("tS", T.matmul(pb[3][0:1, 0:256], ones_c[0:64, 0:1],
                                         sq64[:], start=True, stop=True))
            kb.wait(V, tmk)
            kb.wait(V, rowKR_mk)
            rowKR_mk = kb.bump("vS", V.tensor_copy(out=row(1, 256), in_=pb[3][0:1, 0:256]))
            kb.evac_mark(3, rowKR_mk)
            for h in range(NH):
                bk = (0, 1)[h % 2]
                kb.use_bank(bk)
                kb.wait(T, wres_mk)
                kb.wait(T, ck_mks[-1])
                prs = [(wukv_k[:, l * 2048 + h * 128:l * 2048 + (h + 1) * 128],
                        ckv[:, l * 256:(l + 1) * 256]) for l in range(4)]
                tmk = kb.bump("tS", mm_group(pb[bk][:, 0:256], prs))
                kst = kst2[h % 2]
                kb.wait(SC, tmk)
                kb.wait(SC, prev_kst[h % 2])
                kev = kb.bump("sS", SC.activation(
                    out=kst[:], in_=pb[bk][:, 0:256], func=AF.Identity,
                    bias=biases[:, 72 + h:73 + h]))
                kb.evac_mark(bk, kev)
                kb.wait(SY, kev)
                kship.append(kb.dma(SY, "qS", k_park_d[h][:, g * 256:(g + 1) * 256],
                                    kst[:]))
                prev_kst[h % 2] = kship[-1]
                kb.wait(SC, kev)
                s2 = kb.bump("sS", SC.activation(out=sq_sc[:, 0:256], in_=kst[:],
                                                 func=AF.Square))
                kb.use_bank(3)
                kb.wait(T, s2)
                tmk = kb.bump("tS", T.matmul(pb[3][0:1, 0:256], ones_c[:],
                                             sq_sc[:, 0:256], start=True, stop=True))
                kb.wait(V, tmk)
                kb.wait(V, rowKR_mk)
                vchain(V.tensor_tensor(out=row(0, 256), in0=pb[3][0:1, 0:256],
                                       in1=row(1, 256), op=ALU.add))
                kb.evac_mark(3, kb.mark("vS"))
                vchain(V.tensor_reduce(out=rows[0:1, 512:513], in_=row(0, 256),
                                       axis=AX.X, op=ALU.max))
                if g == 0:
                    km = vchain(V.tensor_copy(out=rows[0:1, 1792 + h:1793 + h],
                                              in_=rows[0:1, 512:513]))
                else:
                    km = vchain(V.tensor_tensor(
                        out=rows[0:1, 1792 + h:1793 + h],
                        in0=rows[0:1, 1792 + h:1793 + h],
                        in1=rows[0:1, 512:513], op=ALU.max))
                kmax_mks[h] = km
            for ts_ in range(2):
                vst = vst2[ts_]
                for dvb in range(4):
                    bk = (4, 5)[dvb % 2]
                    kb.use_bank(bk)
                    kb.wait(T, ck_mks[-1])
                    kb.wait(T, wres_mk)
                    kb.wait(T, vb_l)
                    kb.wait(T, ones_mk)
                    prs = [(ckv[:, l * 256 + ts_ * 128:l * 256 + ts_ * 128 + 128],
                            wukv_v[:, l * 2048 + dvb * 512:l * 2048 + dvb * 512 + 512])
                           for l in range(4)]
                    tmk = kb.bump("tS", mm_group(
                        pb[bk][:, 0:512], prs,
                        bias_k1=(ones_rb[:], vbias[0:1, dvb * 512:dvb * 512 + 512])))
                    kb.wait(SC, tmk)
                    kb.wait(SC, prev_v[ts_])
                    vev = kb.bump("sS", SC.activation(
                        out=vst[:, dvb * 512:dvb * 512 + 512], in_=pb[bk][:, 0:512],
                        func=AF.Copy))
                    kb.evac_mark(bk, vev)
                kb.wait(SY, vev)
                vship.append(kb.dma(
                    SY, "qS",
                    v_park_d[g * 256 + ts_ * 128:g * 256 + ts_ * 128 + 128, :],
                    vst[:]))
                prev_v[ts_] = vship[-1]
            xg_busy[buf] = kb.mark("tS")
        kr_done = kb.mark("vS")
        st3kv.close()

        st3o = ExitStack()
        Ob = sb("Ob", [128, NH * 1024], BF16, st3o)
        st3a = ExitStack()
        khead = [sb(f"khead{i}", [128, S], BF16, st3a) for i in range(2)]
        vhead = [sb(f"vhead{i}", [128, 16 * 128], BF16, st3a) for i in range(2)]
        qh = [sb(f"qh{i}", [128, TPC], BF16, st3a) for i in range(2)]
        qrh = [sb(f"qrh{i}", [64, TPC], BF16, st3a) for i in range(2)]
        T1 = [sb(f"T1_{i}", [128, 1024], F32, st3a) for i in range(2)]
        Pt = [sb(f"Pt_{i}", [128, 1024], BF16, st3a) for i in range(2)]
        qsq = sb("qsq", [64, TPC], BF16, st3a)
        hd_busy = [None, None]
        qn2_tmk = [None, None]
        o_mks = []
        lastk = (7, 15)
        last_exp = {0: None, 1: None}
        last_avden = {0: None, 1: None}
        for h in range(NH):
            buf = h % 2
            kb.wait(SY, hd_busy[buf])
            if h == 0:
                wait_all(SY, kship)
                wait_all(SY, vship)
                wait_all(SY, qship)
                wait_all(SY, qrship)
                kb.wait(SY, kb.mark("tS"))
            klm = kb.dma(SY, "qS", khead[buf][:], k_park_d[h])
            vlm = kb.dma(SY, "qS", vhead[buf][:],
                         bass.AP(v_park_d.tensor, v_park_d.offset + h * 128,
                                 [[D, 128], [128 * D, 16], [1, 128]]))
            qlm = kb.dma(SY, "qS", qh[buf][:], q_park_d[h])
            qrlm = kb.dma(SY, "qS", qrh[buf][:], qr_park_d[h])
            kb.wait(SC, qlm)
            kb.wait(SC, qn2_tmk[1 - buf] if h > 0 else None)
            sq1 = kb.bump("sS", SC.activation(out=sq_sc[:], in_=qh[buf][:], func=AF.Square))
            kb.wait(SC, qrlm)
            sq2 = kb.bump("sS", SC.activation(out=qsq[:], in_=qrh[buf][:], func=AF.Square))
            for half in range(2):
                bk = (6, 7)[half]
                kb.use_bank(bk)
                kb.wait(T, sq1)
                kb.wait(T, sq2)
                T.matmul(pb[bk][0:1, 0:512], ones_c[:],
                         sq_sc[:, half * 512:half * 512 + 512], start=True, stop=False)
                tmk = kb.bump("tS", T.matmul(
                    pb[bk][0:1, 0:512], ones_c[0:64, 0:1],
                    qsq[:, half * 512:half * 512 + 512], start=False, stop=True))
                kb.wait(V, tmk)
                emk = kb.bump("vS", V.tensor_copy(out=row(0, 512, half * 512),
                                                  in_=pb[bk][0:1, 0:512]))
                kb.evac_mark(bk, emk)
            qn2_tmk[buf] = kb.mark("tS")
            qn_mk = kb.mark("vS")
            kb.wait(V, qn_mk)
            kb.wait(V, kmax_mks[h])
            brow = vchain(V.tensor_scalar(
                out=row(0), in0=row(0), scalar1=rows[0:1, 1792 + h:1793 + h],
                scalar2=0.5 * RSQ_DK, op0=ALU.add, op1=ALU.mult))
            b_bc = bcast_row(row(0), bca, brow)
            av_mks = [None, None]
            den_mks = [None, None]
            exp_mks = [None] * 16

            def issue_avden(k, buf=buf):
                kp = k % 2
                W = UC[k]
                c0 = 1024 - W * 128
                kb.wait(T, exp_mks[k])
                for half, bk in ((0, 6), (1, 7)):
                    s0, s1 = half * 512, half * 512 + 512
                    if s1 <= c0:
                        continue
                    a0 = max(s0, c0)
                    if k == 0:
                        kb.use_bank(bk)
                    inst = T.matmul(pb[bk][0:1, a0 - s0:512], ones_c[:],
                                    Pt[kp][:, a0:s1], start=(k == 0),
                                    stop=(k == lastk[half]))
                    if k == lastk[half]:
                        den_mks[half] = kb.bump("tS", inst)
                for half, bk in ((0, 4), (1, 5)):
                    s0, s1 = half * 512, half * 512 + 512
                    if s1 <= c0:
                        continue
                    a0 = max(s0, c0)
                    if k == 0:
                        kb.use_bank(bk)
                    inst = T.matmul(pb[bk][:, a0 - s0:512],
                                    vhead[buf][:, k * 128:(k + 1) * 128],
                                    Pt[kp][:, a0:s1], start=(k == 0),
                                    stop=(k == lastk[half]))
                    if k == lastk[half]:
                        av_mks[half] = kb.bump("tS", inst)
                last_avden[kp] = kb.mark("tS")

            for k in range(16):
                kp = k % 2
                W = UC[k]
                c0 = 1024 - W * 128
                b0, b1 = ((0, 1), (2, 3))[kp]
                segs = []
                if c0 < 512:
                    segs.append((b0, c0, 512))
                segs.append((b1, max(c0, 512), 1024))
                kb.wait(T, klm)
                kb.wait(T, vlm)
                kb.wait(T, qlm)
                kb.wait(T, qrlm)
                if k == 0:
                    kb.wait(T, kr_done)
                sc_mks = []
                for (bk, s0, s1) in segs:
                    kb.use_bank(bk)
                    T.matmul(pb[bk][:, 0:s1 - s0], khead[buf][:, k * 128:(k + 1) * 128],
                             qh[buf][:, s0:s1], start=True, stop=False)
                    tmk = kb.bump("tS", T.matmul(
                        pb[bk][:, 0:s1 - s0], kRt[:, k * 128:(k + 1) * 128],
                        qrh[buf][:, s0:s1], start=False, stop=True))
                    sc_mks.append((bk, s0, s1, tmk))
                kb.wait(V, b_bc)
                kb.wait(V, last_exp[kp])
                t1_mk = None
                for (bk, s0, s1, tmk) in sc_mks:
                    kb.wait(V, tmk)
                    t1_mk = kb.bump("vS", V.scalar_tensor_tensor(
                        out=T1[kp][:, s0:s1], in0=pb[bk][:, 0:s1 - s0], scalar=RSQ_DK,
                        in1=bca[:, s0:s1], op0=ALU.mult, op1=ALU.subtract))
                    kb.evac_mark(bk, t1_mk)
                kb.wait(V, t1_mk)
                kb.wait(V, mload)
                msk = kb.bump("vS", V.tensor_tensor(
                    out=T1[kp][:, c0:c0 + 128], in0=T1[kp][:, c0:c0 + 128],
                    in1=masksb[:, k * 128:(k + 1) * 128], op=ALU.add))
                kb.wait(SC, msk)
                kb.wait(SC, last_avden[kp])
                exp_mks[k] = kb.bump("sS", SC.activation(out=Pt[kp][:, c0:1024],
                                                         in_=T1[kp][:, c0:1024],
                                                         func=AF.Exp))
                last_exp[kp] = exp_mks[k]
                if k >= 1:
                    issue_avden(k - 1)
            issue_avden(15)
            hd_busy[buf] = kb.mark("tS")
            kb.wait(V, den_mks[0])
            kb.wait(V, den_mks[1])
            vchain(V.tensor_copy(out=row(0, 512), in_=pb[6][0:1, 0:512]))
            dn2 = kb.bump("vS", V.tensor_copy(out=row(0, 512, 512), in_=pb[7][0:1, 0:512]))
            kb.evac_mark(6, dn2)
            kb.evac_mark(7, dn2)
            kb.wait(V, dn2)
            rcp = vchain(V.reciprocal(out=row(0), in_=row(0)))
            rd_bc = bcast_row(row(0), bcb, rcp)
            kb.wait(V, rd_bc)
            kb.wait(V, av_mks[0])
            kb.wait(V, av_mks[1])
            om = None
            for half, bk in ((0, 4), (1, 5)):
                om = kb.bump("vS", V.tensor_tensor(
                    out=Ob[:, h * 1024 + half * 512:h * 1024 + half * 512 + 512],
                    in0=pb[bk][:, 0:512], in1=bcb[:, half * 512:half * 512 + 512],
                    op=ALU.mult))
                kb.evac_mark(bk, om)
            o_mks.append(om)
        st3a.close()

        st3w = ExitStack()
        stream3 = sb("stream3", [128, NCH * 1024], F32, st3w)
        kb.wait(SY, kb.mark("tS"))
        rel3 = reload_stream(stream3)
        begin_wphase(kb.mark("tS"), dbl=False)

        def mov_o(i, half):
            return Ob[:, i * 1024 + half * 512:i * 1024 + half * 512 + 512]

        res3 = None
        for e in range(16):
            def evac_res3(half, bk, tmk, e=e):
                kb.wait(V, tmk)
                if e == 0 and half == 0:
                    wait_all(V, rel3)
                    kb.wait(V, o_mks[-1])
                emk = kb.bump("vS", V.scalar_tensor_tensor(
                    out=stream3[:, e * 1024 + half * 512:e * 1024 + half * 512 + 512],
                    in0=pb[bk][:, 0:512], scalar=biases[:, 112 + e:113 + e],
                    in1=stream3[:, e * 1024 + half * 512:e * 1024 + half * 512 + 512],
                    op0=ALU.add, op1=ALU.add))
                kb.evac_mark(bk, emk)
                return emk
            res3 = proj16(e, None, None, wo1_d, e * 128, D, mov_o,
                          o_mks[-1] if e == 0 else None, evac_f=evac_res3)
        end_wphase()
        kb.wait(SC, xship)
        ln3 = layernorm(stream3, res3)
        park3 = park_stream(stream3, ln3)
        st3w.close()
        st3o.close()
        st3.close()

    if lvl >= 4:
        stream4, ln4, st_f4 = ffn(aguA_out[1], aguB_out[1], adn_out[1], ln3,
                                  ccm["agA1"], ccm["agB1"], ccm["agD1"], park3)
        final_stream, final_ln = stream4, ln4
    elif lvl == 3:
        stream_o3 = sb("stream_o3", [128, NCH * 1024], F32)
        wait_all(SY, park3)
        rel_o3 = reload_stream(stream_o3)
        wait_all(SY, rel_o3)
        final_stream, final_ln = stream_o3, None

    if lvl == 1:
        stream2 = sb("stream_out", [128, NCH * 1024], F32)
        wait_all(SY, park1)
        rel = reload_stream(stream2)
        final_stream, final_ln = stream2, None
        wait_all(SY, rel)
    elif lvl == 0:
        final_stream, final_ln = stream, kb.mark("sS")

    if final_ln is not None:
        kb.wait(SY, final_ln)
    kb.wait(SY, kb.mark("vS"))
    kb.wait(SY, kb.mark("sS"))
    st_mks = [kb.dma(SY, "qS", y_d[c * 128:(c + 1) * 128, :],
                     final_stream[:, c * 1024:(c + 1) * 1024]) for c in range(NCH)]
    wait_all(SY, st_mks)

    if lvl == 0:
        st_main.close()
    if lvl in (2,) and not tapped[0]:
        st_f2.close()
    if lvl >= 4:
        st_f4.close()
    es.close()
    return nc


def kernel(**inputs):
    in_maps = make_in_maps(inputs)
    nc = build(_UPTO, _TAP)
    res = run_bass_kernel_spmd(nc, in_maps, core_ids=list(range(8)))
    out = np.zeros((B, S, D), dtype=np.float32)
    for c in range(8):
        out[c // 2, _core_pos(c), :] = np.asarray(res.results[c]["y"]).T
    return out
